# version 02
# baseline (speedup 1.0000x reference)
import numpy as np
import ml_dtypes
import concourse.bass as bass
import concourse.mybir as mybir
from concourse.bass_utils import run_bass_kernel_spmd
from contextlib import ExitStack

F32 = mybir.dt.float32
BF16 = mybir.dt.bfloat16
AF = mybir.ActivationFunctionType
ALU = mybir.AluOpType
AX = mybir.AxisListType
BF = ml_dtypes.bfloat16

SEQ = 4096
D = 2048
NT = SEQ // 128
KT = D // 128
EPS = 1e-5

ENGINES = ("sync", "gpsimd", "scalar", "vector", "tensor")


class Cnt:
    def __init__(self, handle, name):
        self.h = handle
        self.v = 0
        self.name = name

    def inc(self, k=1):
        self.v += k
        return self.v


class Rec:

    def __init__(self):
        self.streams = {k: [] for k in ENGINES}
        self.meta = {k: [] for k in ENGINES}
        self.dma_sems = set()

    def op(self, eng, fn, wait=(), inc=None, flush=None, dma_sem=False):
        w = [(c.h, val) for (c, val) in wait if val > 0]
        if inc is not None:
            cnt, k = inc
            cnt.inc(k)
            tgt = (cnt.h, k)
        else:
            tgt = None
        ftgt = None
        fonly = None
        if flush is not None:
            if inc is not None:
                fonly = (inc[0].h, inc[0].v)
            else:
                flush.inc(1)
                ftgt = (flush.h, flush.v)

        def emit(e, w=w, tgt=tgt, fn=fn, ftgt=ftgt, fonly=fonly):
            for h, val in w:
                e.wait_ge(h, val)
            ins = fn(e)
            if tgt is not None:
                ins.then_inc(tgt[0], tgt[1])
            if ftgt is not None:
                ins.then_inc(ftgt[0], 1)
                e.wait_ge(ftgt[0], ftgt[1])
            if fonly is not None:
                e.wait_ge(fonly[0], fonly[1])
        self.streams[eng].append(emit)
        mw = [(id(c.h), val) for (c, val) in wait if val > 0]
        minc = []
        if inc is not None:
            minc.append((id(inc[0].h), inc[1]))
        if flush is not None and inc is None:
            minc.append((id(flush.h), 1))
        if dma_sem:
            self.dma_sems.add(id(inc[0].h))
        import traceback
        fr = traceback.extract_stack(limit=4)[0]
        self.meta[eng].append((mw, minc, f"{fr.name}:{fr.lineno}"))

    def wait(self, eng, cnt, val):
        if val > 0:
            self.streams[eng].append(lambda e, h=cnt.h, v=val: e.wait_ge(h, v))
            self.meta[eng].append(([(id(cnt.h), val)], [], "barrier"))

    def check(self):
        pos = {k: 0 for k in ENGINES}
        sems = {}
        progress = True
        total = sum(len(v) for v in self.meta.values())
        done = 0
        while progress:
            progress = False
            for eng in ENGINES:
                while pos[eng] < len(self.meta[eng]):
                    mw, minc, tag = self.meta[eng][pos[eng]]
                    if all(sems.get(h, 0) >= v for h, v in mw):
                        for h, k in minc:
                            sems[h] = sems.get(h, 0) + k
                        pos[eng] += 1
                        done += 1
                        progress = True
                    else:
                        break
        if done < total:
            lines = []
            for eng in ENGINES:
                if pos[eng] < len(self.meta[eng]):
                    mw, minc, tag = self.meta[eng][pos[eng]]
                    missing = [(h, v, sems.get(h, 0)) for h, v in mw
                               if sems.get(h, 0) < v]
                    lines.append(f"  {eng} stuck at #{pos[eng]} ({tag}): needs {missing}")
            raise RuntimeError(f"DEADLOCK: {done}/{total} ops executed\n" + "\n".join(lines))
        races = self.race_check()
        if races:
            raise RuntimeError("DMA-SEM RACES:\n" + "\n".join(races[:20]))
        return True

    def race_check(self):
        pos = {k: 0 for k in ENGINES}
        sems = {}
        issued = {}
        races = []
        progress = True
        while progress:
            progress = False
            for eng in ENGINES:
                while pos[eng] < len(self.meta[eng]):
                    mw, minc, tag = self.meta[eng][pos[eng]]
                    if all(sems.get(h, 0) >= v for h, v in mw):
                        for h, v in mw:
                            if h in self.dma_sems and v < 16 * issued.get(h, 0)                                     and v > 0:
                                races.append(
                                    f"  {eng} #{pos[eng]} ({tag}): waits {v} on dma-sem "
                                    f"with {issued.get(h, 0)} issued (needs =={16*issued.get(h,0)})")
                        for h, k in minc:
                            sems[h] = sems.get(h, 0) + k
                            if h in self.dma_sems and k == 16:
                                issued[h] = issued.get(h, 0) + 1
                        pos[eng] += 1
                        progress = True
                    else:
                        break
        return races

    def replay(self, block):
        for name in ENGINES:
            items = self.streams[name]
            if items:
                getattr(block, name)(lambda e, items=items: [f(e) for f in items])


class Stream:

    def __init__(self, env, name, nbuf=1):
        nc, ctx = env["nc"], env["ctx"]
        self.cnts = [Cnt(ctx.enter_context(nc.semaphore(f"{name}{i}")), f"{name}{i}")
                     for i in range(nbuf)]
        self.nbuf = nbuf
        self.i = 0
        self.eng = None
        env.setdefault("_streams", []).append(self)

    def nxt(self):
        c = self.cnts[self.i % self.nbuf]
        self.i += 1
        return c


def dma(rec, eng, out, in_, ld, wait=(), **kw):
    if isinstance(ld, Stream):
        c = ld.nxt()
        ld.eng = eng
    else:
        c = ld
    rec.op(eng, lambda e: e.dma_start(out=out, in_=in_, **kw), wait=wait,
           inc=(c, 16), dma_sem=True)
    return (c, c.v)



def emit_ln(rec, env, x_ap, xln_bf_ap, wait_loaded, norm_extra_wait=()):
    ve, ac = env["ve"], env["ac"]
    t = env
    vf, af = env["vf"], env["af"]
    rec.op("scalar", lambda e: e.activation(t["trash"][:], x_ap, AF.Square,
                                            accum_out=t["sumsq_c"][:]),
           wait=wait_loaded, inc=(ac, 1))
    sq_done = ac.v
    rec.op("vector", lambda e: e.reduce_sum(t["sum_c"][:], x_ap, axis=AX.X),
           wait=wait_loaded, flush=vf)
    rec.op("vector", lambda e: e.tensor_scalar(t["mean_c"][:], t["sum_c"][:], 1.0 / D, None, ALU.mult),
           flush=vf)
    rec.op("vector", lambda e: e.tensor_scalar(t["var_c"][:], t["sumsq_c"][:], 1.0 / D, None, ALU.mult),
           wait=[(ac, sq_done)], flush=vf)
    rec.op("vector", lambda e: e.tensor_mul(t["m2_c"][:], t["mean_c"][:], t["mean_c"][:]), flush=vf)
    rec.op("vector", lambda e: e.tensor_sub(t["var_c"][:], t["var_c"][:], t["m2_c"][:]), inc=(ve, 1))
    var_done = ve.v
    rec.op("scalar", lambda e: e.activation(t["lnv_c"][:], t["var_c"][:], AF.Ln, bias=t["eps_col"][:]),
           wait=[(ve, var_done)], flush=af)
    rec.op("scalar", lambda e: e.activation(t["rstd_c"][:], t["lnv_c"][:], AF.Exp, scale=-0.5),
           inc=(ac, 1))
    rstd_done = ac.v
    rec.op("vector", lambda e: e.tensor_scalar(xln_bf_ap, x_ap, t["mean_c"][:], t["rstd_c"][:],
                                               ALU.subtract, ALU.mult),
           wait=[(ac, rstd_done)] + list(norm_extra_wait), inc=(ve, 1))
    return ve.v


def emit_transpose_pack(rec, env, src_ap_fn, nblk, dst_ap_fn, bf, wait_first=()):
    ve, pe = env["ve"], env["pe"]
    ident = env["ident_sb"] if bf else env["identf_sb"]
    banks = (env["ps_tra"], env["ps_trb"]) if bf else (env["ps_fra"], env["ps_frb"])
    bkey = "bf" if bf else "fr"
    bpb = 8 if bf else 4
    nbank = (nblk + bpb - 1) // bpb
    for b in range(nbank):
        bank = banks[b % 2]
        lo, hi = b * bpb, min(b * bpb + bpb, nblk)
        nb = hi - lo
        w = list(wait_first) if b < 2 else []
        prev = env["tr_bank_ve"].get((bkey, b % 2))
        if prev:
            w.append((env["ve"], prev))
        for j in range(lo, hi):
            rec.op("tensor", lambda e, j=j, lo=lo, bank=bank: e.transpose(
                bank[:, (j - lo) * 128:(j - lo) * 128 + 128], src_ap_fn(j), ident[:]),
                wait=(w if j == lo else ()),
                inc=((pe, 1) if j == hi - 1 else None))
        pe_done = pe.v
        rec.op("vector", lambda e, bank=bank, nb=nb, b=b: e.tensor_copy(
            dst_ap_fn(b, nb),
            bank[:, 0:nb * 128].rearrange("p (j c) -> p j c", j=nb)),
            wait=[(pe, pe_done)], inc=(ve, 1), flush=env["vf"])
        env["tr_bank_ve"][(bkey, b % 2)] = ve.v
    return ve.v, pe.v


def common_env(nc, ctx):
    sb = lambda name, shape, dt=F32: ctx.enter_context(nc.sbuf_tensor(name, shape, dt))
    psa = lambda name, shape, dt=F32: ctx.enter_context(nc.psum_tensor(name, shape, dt))
    env = {"tr_bank_ve": {}}
    env["ident_sb"] = sb("ident_sb", [128, 128], BF16)
    env["identf_sb"] = sb("identf_sb", [128, 128], F32)
    env["ones_sb"] = sb("ones_sb", [1, 128])
    for n in ("sum_c", "sumsq_c", "mean_c", "var_c", "m2_c", "lnv_c", "rstd_c"):
        env[n] = sb(n, [128, 1])
    env["trash"] = sb("trash", [128, D], BF16)
    env["eps_col"] = sb("eps_col", [128, 1])
    env["ps_tra"] = psa("ps_tra", [128, 1024], BF16)
    env["ps_trb"] = psa("ps_trb", [128, 1024], BF16)
    env["ps_fra"] = psa("ps_fra", [128, 512], F32)
    env["ps_frb"] = psa("ps_frb", [128, 512], F32)
    env["ps_a"] = psa("ps_a", [128, 512], F32)
    env["ps_b"] = psa("ps_b", [128, 512], F32)
    env["ps_c"] = psa("ps_c", [128, 512], F32)
    env["ps_d"] = psa("ps_d", [128, 512], F32)
    sems = {}
    for s in ("pe", "ve", "ac", "po", "vf", "af"):
        sems[s] = Cnt(ctx.enter_context(nc.semaphore(s)), s)
    env.update(sems)
    env["nc"] = nc
    env["ctx"] = ctx
    env["_streams"] = []
    env["bar"] = Cnt(ctx.enter_context(nc.semaphore("bar")), "bar")
    env["s_const"] = Stream(env, "cst", 1)
    return env, sems, sb, psa


def emit_load_consts(rec, env, ident_bf, identf, ones_f):
    s = env["s_const"]
    dma(rec, "sync", env["ident_sb"][:], ident_bf[:], s)
    dma(rec, "sync", env["identf_sb"][:], identf[:], s)
    tok = dma(rec, "sync", env["ones_sb"][:], ones_f[:], s)
    rec.op("vector", lambda e: e.memset(env["eps_col"][:], EPS), inc=(env["ve"], 1))
    return tok


class PhaseAlloc:

    def __init__(self, arena_ap, limit):
        self.ap = arena_ap
        self.limit = limit
        self.off = 0
        self.hwm = 0

    def reset(self):
        self.hwm = max(self.hwm, self.off)
        self.off = 0

    def sb(self, name, shape, dt=F32):
        esz = {F32: 4, BF16: 2}.get(dt, 4)
        nb = int(np.prod(shape[1:])) * esz
        nba = (nb + 15) // 16 * 16
        assert self.off + nba <= self.limit, (name, self.off, nb, self.limit)
        ap = self.ap[0:shape[0], self.off:self.off + nb].bitcast(dt)
        self.off += nba
        if len(shape) == 3:
            ap = ap.rearrange("p (a c) -> p a c", a=shape[1])
        return ap


def emit_barrier(rec, env):
    bar = env["bar"]
    gp_tok = [(c, c.v) for s in env["_streams"] if s.eng == "gpsimd" for c in s.cnts]
    sy_tok = [(c, c.v) for s in env["_streams"] if s.eng == "sync" for c in s.cnts]
    eng_tok = [(env[k], env[k].v) for k in ("pe", "ve", "ac", "po", "vf", "af")]
    rec.op("gpsimd", lambda e: e.nop(), wait=gp_tok, inc=(bar, 1))
    rec.op("sync", lambda e: e.nop(),
           wait=sy_tok + eng_tok + [(bar, bar.v)], inc=(bar, 1))
    bv = bar.v
    for eng in ENGINES:
        rec.wait(eng, bar, bv)


def mk_consts(nc):
    ident_bf = nc.inline_tensor(np.eye(128, dtype=BF), name="c_ident_bf").ap()
    identf = nc.inline_tensor(np.eye(128, dtype=np.float32), name="c_identf").ap()
    ones_f = nc.inline_tensor(np.ones((1, 128), np.float32), name="c_ones_f").ap()
    return ident_bf, identf, ones_f



def emit_ffn0(rec, env, aps, sb, psa):
    nc_x, w, bias, xo = aps["x"], aps["w"], aps["bias"], aps["xo"]
    pe, ve, ac = (env[k] for k in ("pe", "ve", "ac"))
    nc_, ctx_ = env["nc"], env["ctx"]
    s_x = Stream(env, "f0x", 2)
    s_w = Stream(env, "f0w", 2)
    s_o = Stream(env, "f0o", 2)

    bias_row = sb("f0_bias_row", [1, D])
    bias_tile = sb("f0_bias_tile", [128, D])
    wn = [sb(f"f0_wn{i}", [128, D]) for i in range(2)]
    alpha = sb("f0_alpha", [128, KT])
    sgn = sb("f0_sgn", [128, D])
    bw_row = sb("f0_bw_row", [128, D], BF16)
    bwT = sb("f0_bwT", [128, KT * D], BF16)
    x_sb = [sb(f"f0_x{i}", [128, D]) for i in range(2)]
    out_sb = [sb(f"f0_out{i}", [128, D]) for i in range(2)]
    xln = sb("f0_xln", [128, D], BF16)
    xlnT = sb("f0_xlnT", [128, D], BF16)
    ps_mm = [env["ps_a"], env["ps_b"]]

    bwT_kc = bwT[:].rearrange("p (kc o) -> p kc o", kc=KT)

    bias_tok = dma(rec, "sync", bias_row[:], bias.rearrange("(one d) -> one d", one=1),
                   env["s_const"])

    for ns in range(4):
        rec.op("tensor", lambda e, ns=ns: e.matmul(
            env["ps_c"][:], env["ones_sb"][:],
            bias_row[0:1, ns * 512:(ns + 1) * 512], start=True, stop=True),
            wait=([bias_tok] if ns == 0 else [(ve, ve.v)]),
            inc=(pe, 1))
        rec.op("vector", lambda e, ns=ns: e.tensor_copy(
            bias_tile[:, ns * 512:(ns + 1) * 512], env["ps_c"][:]),
            wait=[(pe, pe.v)], inc=(ve, 1))

    w_released = []
    bw_pe_done = 0
    for i in range(KT):
        wt = [(ve, w_released[i - 2])] if i >= 2 else []
        w_tok = dma(rec, "gpsimd", wn[i % 2][:], w[i * 128:(i + 1) * 128, :], s_w, wait=wt)
        rec.op("scalar", lambda e, i=i: e.activation(
            env["trash"][:], wn[i % 2][:], AF.Abs, accum_out=alpha[:, i:i + 1]),
            wait=[w_tok], inc=(ac, 1))
        rec.op("vector", lambda e, i=i: e.tensor_scalar(
            sgn[:], wn[i % 2][:], 0.0, 2.0, ALU.is_ge, ALU.mult),
            wait=[(ac, ac.v)] + ([(pe, bw_pe_done)] if bw_pe_done else []), flush=env["vf"])
        rec.op("vector", lambda e: e.tensor_scalar(sgn[:], sgn[:], 1.0, None, ALU.subtract), flush=env["vf"])
        rec.op("vector", lambda e, i=i: e.tensor_scalar(
            alpha[:, i:i + 1], alpha[:, i:i + 1], 1.0 / D, None, ALU.mult), flush=env["vf"])
        rec.op("vector", lambda e, i=i: e.tensor_scalar(
            bw_row[:], sgn[:], alpha[:, i:i + 1], None, ALU.mult), inc=(ve, 1))
        w_released.append(ve.v)
        bw_done = ve.v
        _, bw_pe_done = emit_transpose_pack(
            rec, env,
            src_ap_fn=lambda j: bw_row[:, j * 128:(j + 1) * 128],
            nblk=KT,
            dst_ap_fn=lambda b, nb, i=i: bwT_kc[:, b * 8:b * 8 + nb, i * 128:(i + 1) * 128],
            bf=True, wait_first=[(ve, bw_done)])
    wprep_ve = ve.v

    x_released = {}
    out_stored = {}
    evac_done = {0: 0, 1: 0}
    xln_pe_done = 0
    xlnT_consumed_pe = 0
    for t in range(NT):
        wt = [(ve, x_released[t - 2])] if t >= 2 else []
        x_tok = dma(rec, "sync", x_sb[t % 2][:], nc_x[t * 128:(t + 1) * 128, :], s_x, wait=wt)
        wl = [x_tok]
        ln_done = emit_ln(rec, env, x_sb[t % 2][:], xln[:], wl,
                          norm_extra_wait=([(pe, xln_pe_done)] if xln_pe_done else ()))
        tr_done, xln_pe_done = emit_transpose_pack(
            rec, env,
            src_ap_fn=lambda j: xln[:, j * 128:(j + 1) * 128],
            nblk=KT,
            dst_ap_fn=lambda b, nb: xlnT[:, b * 1024:b * 1024 + nb * 128].rearrange(
                "p (j c) -> p j c", j=nb),
            bf=True,
            wait_first=[(ve, ln_done), (ve, wprep_ve), (pe, xlnT_consumed_pe)])
        for ns in range(4):
            pw = [(ve, tr_done)]
            if evac_done[ns % 2]:
                pw.append((ve, evac_done[ns % 2]))
            for kc in range(KT):
                rec.op("tensor", lambda e, ns=ns, kc=kc: e.matmul(
                    ps_mm[ns % 2][:],
                    xlnT[:, kc * 128:(kc + 1) * 128],
                    bwT[:, kc * D + ns * 512: kc * D + (ns + 1) * 512],
                    start=(kc == 0), stop=(kc == KT - 1)),
                    wait=(pw if kc == 0 else ()),
                    inc=((pe, 1) if kc == KT - 1 else None))
            mm_done = pe.v
            if ns == 3:
                xlnT_consumed_pe = pe.v
            ow = [(pe, mm_done)]
            if t >= 2 and ns == 0:
                ow.append(out_stored[t - 2])
            rec.op("vector", lambda e, t=t, ns=ns: e.tensor_add(
                out_sb[t % 2][:, ns * 512:(ns + 1) * 512], ps_mm[ns % 2][:],
                bias_tile[:, ns * 512:(ns + 1) * 512]), wait=ow)
            rec.op("vector", lambda e, t=t, ns=ns: e.tensor_add(
                out_sb[t % 2][:, ns * 512:(ns + 1) * 512],
                out_sb[t % 2][:, ns * 512:(ns + 1) * 512],
                x_sb[t % 2][:, ns * 512:(ns + 1) * 512]), inc=(ve, 1))
            evac_done[ns % 2] = ve.v
        x_released[t] = ve.v
        out_stored[t] = dma(rec, "sync", xo[t * 128:(t + 1) * 128, :], out_sb[t % 2][:],
                            s_o, wait=[(ve, ve.v)])
    for cnt in s_o.cnts:
        rec.wait("sync", cnt, cnt.v)


def build_ffn0():
    nc = bass.Bass("TRN2", target_bir_lowering=False, debug=False)
    aps = {
        "x": nc.dram_tensor("x", [SEQ, D], F32, kind="ExternalInput").ap(),
        "w": nc.dram_tensor("w", [D, D], F32, kind="ExternalInput").ap(),
        "bias": nc.dram_tensor("bias", [D], F32, kind="ExternalInput").ap(),
        "xo": nc.dram_tensor("xo", [SEQ, D], F32, kind="ExternalOutput").ap(),
    }
    ident_bf, identf, ones_f = mk_consts(nc)
    ctx = ExitStack()
    with ctx:
        env, sems, sb, psa = common_env(nc, ctx)
        rec = Rec()
        emit_load_consts(rec, env, ident_bf, identf, ones_f)
        emit_ffn0(rec, env, aps, sb, psa)
        rec.check()
        block = ctx.enter_context(nc.Block())
        rec.replay(block)
    return nc



MI = 1408
MKT = MI // 128
NE = 8


def emit_moe(rec, env, aps, sb, psa, nt=NT):
    x, gwt = aps["x"], aps["gwt"]
    w1t, w3t, w2t = aps["w1t"], aps["w3t"], aps["w2t"]
    sw1t, sw3t, sw2t = aps["sw1t"], aps["sw3t"], aps["sw2t"]
    xo = aps["xo"]
    s_xlnT = aps["s_xlnT"]
    pe, ve, ac, vf, af = (env[k] for k in ("pe", "ve", "ac", "vf", "af"))
    nc_, ctx_ = env["nc"], env["ctx"]
    s_x = Stream(env, "m_x", 1)
    s_xi = Stream(env, "m_xi", 1)
    s_xs = Stream(env, "m_xs", 1)
    s_wh = Stream(env, "m_wh", 1)
    s_w2 = Stream(env, "m_w2", 1)
    s_xt = Stream(env, "m_xt", 2)
    s_ya = Stream(env, "m_ya", 2)

    gwT_sb = sb("m_gwT", [128, KT, 8])
    x_sb = [sb("m_x0", [128, D])]
    x_sb = [x_sb[0], x_sb[0]]
    xlnf = sb("m_xlnf", [128, D])
    xln_bf = sb("m_xln", [128, D], BF16)
    xlnT_bf = sb("m_xlnT", [128, D], BF16)
    xlnfT = sb("m_xlnfT", [128, D])
    sm = sb("m_sm", [128, 8])
    exl = sb("m_exl", [128, 8])
    dw_all = sb("m_dw", [128, nt * 8])
    rmax = sb("m_rmax", [128, 1])
    negmax = sb("m_negmax", [128, 1])
    ssum = sb("m_ssum", [128, 1])
    rs = sb("m_rs", [128, 1])
    eqm = sb("m_eqm", [128, 8])
    masked = sb("m_masked", [128, 8])
    keep = sb("m_keep", [128, 8])
    w1sb = sb("m_w1", [128, KT, MI], BF16)
    w3sb = sb("m_w3", [128, KT, MI], BF16)
    w2sb = sb("m_w2", [128, MKT, D], BF16)
    xt_sb = [sb(f"m_xt{i}", [128, D], BF16) for i in range(2)]
    sil = sb("m_sil", [128, 512])
    h2s = sb("m_h2s", [128, MI], BF16)
    h2sT = sb("m_h2sT", [128, MI], BF16)
    ycont = [sb(f"m_yc{i}", [128, D]) for i in range(2)]

    ps_logit = env["ps_c"]
    ps_h1 = [env["ps_a"], env["ps_c"]]
    ps_h3 = [env["ps_b"], env["ps_d"]]
    ps_y = [env["ps_fra"], env["ps_frb"]]

    gw_tok = dma(rec, "sync", gwT_sb[:], gwt.rearrange("(kc p) e -> p kc e", p=128),
                 env["s_const"])

    xlnT_stored = {}
    xo_init_so = {}
    x_rel = {}
    logit_pe = 0
    xlnT_pe = 0
    for t in range(nt):
        wt = [(ve, x_rel[t - 1]), xo_init_so[t - 1]] if t >= 1 else []
        x_tok = dma(rec, "sync", x_sb[t % 2][:], x[t * 128:(t + 1) * 128, :], s_x, wait=wt)
        wl = [x_tok]
        emit_ln(rec, env, x_sb[t % 2][:], xlnf[:], wl,
                norm_extra_wait=([(env["pe"], env["m_xlnf_pe"])] if env.get("m_xlnf_pe") else ()))
        xo_init_so[t] = dma(rec, "sync", xo[t * 128:(t + 1) * 128, :], x_sb[t % 2][:],
                            s_xi, wait=[(ve, ve.v)])
        x_rel[t] = ve.v
        rec.op("vector", lambda e: e.tensor_copy(xln_bf[:], xlnf[:]),
               wait=([(pe, env["m_xlnbf_pe"])] if env.get("m_xlnbf_pe") else ()),
               flush=vf, inc=(ve, 1))
        cast_done = ve.v
        trv, trpe = emit_transpose_pack(
            rec, env, src_ap_fn=lambda j: xln_bf[:, j * 128:(j + 1) * 128],
            nblk=KT,
            dst_ap_fn=lambda b, nb: xlnT_bf[:, b * 1024:b * 1024 + nb * 128].rearrange(
                "p (j c) -> p j c", j=nb),
            bf=True, wait_first=[(ve, cast_done)] +
                ([xlnT_stored[t - 1]] if t >= 1 else []))
        env["m_xlnbf_pe"] = trpe
        xlnT_stored[t] = dma(rec, "sync", s_xlnT[t, :, :], xlnT_bf[:], s_xs,
                             wait=[(ve, trv)])
        trvf, trpef = emit_transpose_pack(
            rec, env, src_ap_fn=lambda j: xlnf[:, j * 128:(j + 1) * 128],
            nblk=KT,
            dst_ap_fn=lambda b, nb: xlnfT[:, b * 512:b * 512 + nb * 128].rearrange(
                "p (j c) -> p j c", j=nb),
            bf=False, wait_first=[(ve, ve.v), (pe, logit_pe)])
        env["m_xlnf_pe"] = trpef
        for kc in range(KT):
            rec.op("tensor", lambda e, kc=kc: e.matmul(
                ps_logit[:, 0:8], xlnfT[:, kc * 128:(kc + 1) * 128],
                gwT_sb[:, kc, :], start=(kc == 0), stop=(kc == KT - 1)),
                wait=([(ve, trvf), gw_tok, (ve, env.get("m_dw_ve", 0)), (ac, env.get("m_exp_ac", 0))] if kc == 0 else ()),
                inc=((pe, 1) if kc == KT - 1 else None))
        logit_pe = pe.v
        rec.op("vector", lambda e: e.reduce_max(rmax[:], ps_logit[:, 0:8], axis=AX.X),
               wait=[(pe, logit_pe)], flush=vf)
        rec.op("vector", lambda e: e.tensor_scalar(negmax[:], rmax[:], -1.0, None, ALU.mult),
               flush=vf, inc=(ve, 1))
        rec.op("scalar", lambda e: e.activation(exl[:], ps_logit[:, 0:8], AF.Exp,
                                                bias=negmax[:]),
               wait=[(ve, ve.v)], flush=af, inc=(ac, 1))
        env["m_exp_ac"] = ac.v
        rec.op("vector", lambda e: e.reduce_sum(ssum[:], exl[:], axis=AX.X),
               wait=[(ac, ac.v)], flush=vf)
        rec.op("vector", lambda e: e.reciprocal(rs[:], ssum[:]), flush=vf)
        rec.op("vector", lambda e: e.tensor_scalar(sm[:], exl[:], rs[:], None, ALU.mult),
               flush=vf)
        rec.op("vector", lambda e: e.reduce_max(rmax[:], sm[:], axis=AX.X), flush=vf)
        rec.op("vector", lambda e: e.tensor_scalar(eqm[:], sm[:], rmax[:], None, ALU.is_equal),
               flush=vf)
        rec.op("vector", lambda e: e.scalar_tensor_tensor(
            masked[:], eqm[:], -1e30, sm[:], ALU.mult, ALU.add), flush=vf)
        rec.op("vector", lambda e: e.reduce_max(rmax[:], masked[:], axis=AX.X), flush=vf)
        rec.op("vector", lambda e: e.tensor_scalar(keep[:], sm[:], rmax[:], None, ALU.is_ge),
               flush=vf)
        rec.op("vector", lambda e, t=t: e.tensor_mul(dw_all[:, t * 8:(t + 1) * 8], sm[:], keep[:]),
               inc=(ve, 1), flush=vf)
        env["m_dw_ve"] = ve.v
    m0_ve = ve.v
    xlnT_all_tok = xlnT_stored[nt - 1]
    init_all_tok = xo_init_so[nt - 1]
    if "dbg_dw" in aps:
        dma(rec, "sync", aps["dbg_dw"][:], dw_all[:, 0:nt * 8], s_xi, wait=[(ve, m0_ve)])

    passes = []
    for e in range(NE):
        passes.append((w1t[e], w3t[e], w2t[e], e))
    passes.append((sw1t[:, 0:MI], sw3t[:, 0:MI], sw2t[0:MI, :], None))
    passes.append((sw1t[:, MI:2 * MI], sw3t[:, MI:2 * MI], sw2t[MI:2 * MI, :], None))
    which = aps.get("which")
    if which is not None:
        passes = [passes[i] for i in which]

    slabs3 = [(0, 512), (512, 512), (1024, 384)]
    last_pass_pe = 0
    xt_rel = {}
    yc_rel = {}
    for pi, (pw1, pw3, pw2, dwi) in enumerate(passes):
        wwt = [(pe, env.get("m_hdone", 0))] if pi else [(ve, m0_ve)]
        dma(rec, "gpsimd", w1sb[:], pw1.rearrange("(kc p) i -> p kc i", p=128), s_wh, wait=wwt)
        wh_tok = dma(rec, "gpsimd", w3sb[:], pw3.rearrange("(kc p) i -> p kc i", p=128), s_wh)
        w2_tok = dma(rec, "gpsimd", w2sb[:], pw2.rearrange("(kc p) o -> p kc o", p=128), s_w2,
                     wait=([(pe, last_pass_pe)] if pi else []))
        for t in range(nt):
            g = pi * nt + t
            wt = [xlnT_all_tok]
            if g >= 2:
                wt.append((pe, xt_rel[g - 2]))
            xt_tok = dma(rec, "sync", xt_sb[t % 2][:], s_xlnT[t, :, :], s_xt, wait=wt)
            h2s_parts_ve = []
            for si, (s0, sw) in enumerate(slabs3):
                q = si % 2
                g2 = pi * nt * 3 + t * 3 + si
                mmw = [xt_tok, wh_tok]
                if env.get(("m_ps_h_ve", q), 0):
                    mmw.append((ve, env[("m_ps_h_ve", q)]))
                if env.get(("m_ps_h_ac", q), 0):
                    mmw.append((ac, env[("m_ps_h_ac", q)]))
                for kc in range(KT):
                    rec.op("tensor", lambda e, kc=kc, s0=s0, sw=sw, t=t, q=q: e.matmul(
                        ps_h1[q][:, 0:sw], xt_sb[t % 2][:, kc * 128:(kc + 1) * 128],
                        w1sb[:, kc, s0:s0 + sw],
                        start=(kc == 0), stop=(kc == KT - 1)),
                        wait=(mmw if kc == 0 else ()))
                for kc in range(KT):
                    rec.op("tensor", lambda e, kc=kc, s0=s0, sw=sw, t=t, q=q: e.matmul(
                        ps_h3[q][:, 0:sw], xt_sb[t % 2][:, kc * 128:(kc + 1) * 128],
                        w3sb[:, kc, s0:s0 + sw],
                        start=(kc == 0), stop=(kc == KT - 1)),
                        inc=((pe, 1) if kc == KT - 1 else None))
                hmm_done = pe.v
                rec.op("scalar", lambda e, sw=sw, q=q: e.activation(
                    sil[:, 0:sw], ps_h1[q][:, 0:sw], AF.Silu),
                       wait=[(pe, hmm_done), (ve, env.get(("m_ps_h_ve", q), 0))], inc=(ac, 1))
                env[("m_ps_h_ac", q)] = ac.v
                if dwi is not None:
                    rec.op("vector", lambda e, s0=s0, sw=sw, t=t, dwi=dwi, q=q: e.scalar_tensor_tensor(
                        h2s[:, s0:s0 + sw], sil[:, 0:sw], dw_all[:, t * 8 + dwi:t * 8 + dwi + 1],
                        ps_h3[q][:, 0:sw], ALU.mult, ALU.mult),
                        wait=[(ac, ac.v)], inc=(ve, 1))
                else:
                    rec.op("vector", lambda e, s0=s0, sw=sw, q=q: e.tensor_mul(
                        h2s[:, s0:s0 + sw], sil[:, 0:sw], ps_h3[q][:, 0:sw]),
                        wait=[(ac, ac.v)], inc=(ve, 1))
                env[("m_ps_h_ve", q)] = ve.v
                h2s_parts_ve.append(ve.v)
            env["m_hdone"] = pe.v

            trv, trpe = emit_transpose_pack(
                rec, env, src_ap_fn=lambda j: h2s[:, j * 128:(j + 1) * 128],
                nblk=MKT,
                dst_ap_fn=lambda b, nb: h2sT[:, b * 1024:b * 1024 + nb * 128].rearrange(
                    "p (j c) -> p j c", j=nb),
                bf=True, wait_first=[(ve, h2s_parts_ve[-1])])
            for ns in range(4):
                q = ns % 2
                yw = [(ve, trv), w2_tok]
                if env.get(("m_ps_y_ve", q), 0):
                    yw.append((ve, env[("m_ps_y_ve", q)]))
                if pi == 0 and t == 0 and ns < 2:
                    yw.append((ve, m0_ve))
                for kc in range(MKT):
                    rec.op("tensor", lambda e, kc=kc, ns=ns, q=q: e.matmul(
                        ps_y[q][:], h2sT[:, kc * 128:(kc + 1) * 128],
                        w2sb[:, kc, ns * 512:(ns + 1) * 512],
                        start=(kc == 0), stop=(kc == MKT - 1)),
                        wait=(yw if kc == 0 else ()),
                        inc=((pe, 1) if kc == MKT - 1 else None))
                ymm = pe.v
                ycw = [(pe, ymm)]
                if g >= 2 and ns == 0:
                    ycw.append(yc_rel[g - 2])
                rec.op("vector", lambda e, ns=ns, t=t, q=q: e.tensor_copy(
                    ycont[t % 2][:, ns * 512:(ns + 1) * 512], ps_y[q][:]),
                    wait=ycw, inc=(ve, 1))
                env[("m_ps_y_ve", q)] = ve.v
            env["m_ps_y_ve"] = ve.v
            xt_rel[g] = pe.v
            yc_rel[g] = dma(rec, "gpsimd", xo[t * 128:(t + 1) * 128, :],
                            ycont[t % 2][:], s_ya,
                            wait=[(ve, env["m_ps_y_ve"])] +
                                 ([init_all_tok] if pi == 0 else []),
                            accum_op=ALU.add)
        last_pass_pe = pe.v

    for cnt in s_ya.cnts:
        rec.wait("gpsimd", cnt, cnt.v)


def build_moe(nt=NT, dbg=False, which=None):
    nc = bass.Bass("TRN2", target_bir_lowering=False, debug=False)
    aps = {
        "x": nc.dram_tensor("x", [SEQ, D], F32, kind="ExternalInput").ap(),
        "gwt": nc.dram_tensor("gwt", [D, 8], F32, kind="ExternalInput").ap(),
        "w1t": nc.dram_tensor("w1t", [NE, D, MI], F32, kind="ExternalInput").ap(),
        "w3t": nc.dram_tensor("w3t", [NE, D, MI], F32, kind="ExternalInput").ap(),
        "w2t": nc.dram_tensor("w2t", [NE, MI, D], F32, kind="ExternalInput").ap(),
        "sw1t": nc.dram_tensor("sw1t", [D, 2 * MI], F32, kind="ExternalInput").ap(),
        "sw3t": nc.dram_tensor("sw3t", [D, 2 * MI], F32, kind="ExternalInput").ap(),
        "sw2t": nc.dram_tensor("sw2t", [2 * MI, D], F32, kind="ExternalInput").ap(),
        "xo": nc.dram_tensor("xo", [SEQ, D], F32, kind="ExternalOutput").ap(),
        "s_xlnT": nc.dram_tensor("s_xlnT", [NT, 128, D], BF16, kind="Internal").ap(),
    }
    if dbg:
        aps["dbg_dw"] = nc.dram_tensor("dbg_dw", [128, nt * 8], F32, kind="ExternalOutput").ap()
    aps["which"] = which
    ident_bf, identf, ones_f = mk_consts(nc)
    ctx = ExitStack()
    with ctx:
        env, sems, sb, psa = common_env(nc, ctx)
        rec = Rec()
        emit_load_consts(rec, env, ident_bf, identf, ones_f)
        emit_moe(rec, env, aps, sb, psa, nt=nt)
        rec.check()
        block = ctx.enter_context(nc.Block())
        rec.replay(block)
    return nc



INTER = 4096
NH = 64
HD = 64
NST = 128
CHUNK = 256
NCH = SEQ // CHUNK
CDIM = 4352
HS_T = 34
GATE_T = 32


def mamba_consts(nc, tag):
    trim0 = np.zeros((128, 256), dtype=BF)
    trim1 = np.zeros((128, 256), dtype=BF)
    for p in range(128):
        trim0[p, p:] = 1
        trim1[p, 128 + p:] = 1
    tri0f = np.triu(np.ones((128, 128), np.float32))
    onesf = np.ones((128, 128), np.float32)
    return {
        "trim0": nc.inline_tensor(trim0, name=f"{tag}_trim0").ap(),
        "trim1": nc.inline_tensor(trim1, name=f"{tag}_trim1").ap(),
        "tri0f": nc.inline_tensor(tri0f, name=f"{tag}_tri0f").ap(),
        "onesf": nc.inline_tensor(onesf, name=f"{tag}_onesf").ap(),
    }


def emit_mamba(rec, env, aps, sb, psa, variant, cst, nt=NT, phases="abc", scan_chunks=None, nheads=NH, do_post=True, upto=99):
    x, w_in, conv_w, conv_b = aps["x"], aps["w_in"], aps["conv_w"], aps["conv_b"]
    dt_bias, a_log, dvec, norm_w, w_out = (aps[k] for k in
                                           ("dt_bias", "a_log", "dvec", "norm_w", "w_out"))
    xo = aps["xo"]
    s_hsT, s_gateT, s_dt, s_ynT, s_acum = (aps[k] for k in
                                           ("s_hsT", "s_gateT", "s_dt", "s_ynT", "s_acum"))
    pe, ve, ac, vf, af = (env[k] for k in ("pe", "ve", "ac", "vf", "af"))
    po = env["po"]
    nc_, ctx_ = env["nc"], env["ctx"]
    _sp = "mb" + variant
    st_xa = Stream(env, _sp + "xa", 2)
    st_wf = Stream(env, _sp + "wf", 2)
    st_fs = Stream(env, _sp + "fs", 2)
    st_ds = Stream(env, _sp + "ds", 1)
    st_hs = Stream(env, _sp + "hs", 1)
    st_dt = Stream(env, _sp + "dt", 1)
    st_as = Stream(env, _sp + "as", 1)
    st_ar = Stream(env, _sp + "ar", 2)
    st_gt = Stream(env, _sp + "gt", 1)
    st_ys = Stream(env, _sp + "ys", 1)
    st_wk = Stream(env, _sp + "wk", 2)
    st_yt = Stream(env, _sp + "yt", 2)
    st_xc = Stream(env, _sp + "xc", 2)
    st_os = Stream(env, _sp + "os", 2)
    V = variant
    tg = "mb" + V

    arena_bf = sb(tg + "_abf", [128, 45056], BF16)
    arena_f = sb(tg + "_af", [128, 8704])

    def abf(off, shape):
        n = int(np.prod(shape[1:]))
        ap = arena_bf[0:shape[0], off:off + n]
        if len(shape) == 3:
            ap = ap.rearrange("p (a b) -> p a b", a=shape[1])
        return ap

    def afr(off, shape):
        n = int(np.prod(shape[1:]))
        ap = arena_f[0:shape[0], off:off + n]
        if len(shape) == 3:
            ap = ap.rearrange("p (a b) -> p a b", a=shape[1])
        return ap

    xlnT_blk = abf(0, [128, KT, 2048])
    wfc = [abf(32768 + i * 2048, [128, KT, 128]) for i in range(2)]
    fme = [abf(36864 + i * 512, [128, 512]) for i in range(2)]
    xa = [afr(i * 2048, [128, 2048]) for i in range(2)]
    dte = afr(4096, [64, 512])
    hs_sb = abf(0, [128, HS_T, 259])
    ynT = abf(0, [128, GATE_T, 256])
    xsbc = abf(8832, [128, HS_T, 256])
    gate_fm = abf(8832, [128, GATE_T, 256])
    B_fm = xsbc[:, 32, :]
    C_fm = xsbc[:, 33, :]
    X_tm = [abf(17536 + i * 4096, [128, 4096]) for i in range(2)]
    xdt_tm = [abf(25728 + i * 4096, [128, 4096]) for i in range(2)]
    Xdec_tm = [abf(33920 + i * 4096, [128, 4096]) for i in range(2)]
    gate_tm = Xdec_tm
    conv_acc = afr(0, [128, HS_T, 256])
    y_tm = [afr(i * 4096, [128, 4096]) for i in range(2)]
    wns = abf(0, [128, 32, 512])
    ynTt = [abf(16384 + i * 4096, [128, 32, 128]) for i in range(2)]
    xcs = [afr(i * 512, [128, 512]) for i in range(2)]
    outs = [afr(1024 + i * 512, [128, 512]) for i in range(2)]
    wkf = [afr(2048 + i * 512, [128, 512]) for i in range(2)]

    acol = sb(tg + "_acol", [64, 1])
    dtbcol = sb(tg + "_dtb", [64, 1])
    dtc = sb(tg + "_dtc", [64, 256])
    spf = sb(tg + "_spf", [64, 256])
    decf = sb(tg + "_decf", [64, 256])
    ddecf = sb(tg + "_ddecf", [64, 256])
    acumf = sb(tg + "_acumf", [64, 256])
    adt_tm = sb(tg + "_adttm", [128, 128])
    acum_tm = sb(tg + "_acumtm", [128, 128])
    nacum_tm = sb(tg + "_nacumtm", [128, 128])
    sp_tm = sb(tg + "_sptm", [128, 128])
    ddec_tm = sb(tg + "_ddectm", [128, 128])
    tsb = [sb(tg + f"_tsb{p}", [128, 512]) for p in range(2)]
    ehb = [sb(tg + f"_ehb{p}", [128, 512], BF16) for p in range(2)]
    Mh = [[sb(tg + f"_mh{p}{i}", [128, 256], BF16) for i in range(2)] for p in range(2)]
    exn = [sb(tg + f"_exn{p}", [128, 256]) for p in range(2)]
    chp = [sb(tg + f"_chp{p}", [128, 256], BF16) for p in range(2)]
    scoresm = [sb(tg + f"_scm{i}", [128, 256], BF16) for i in range(2)]
    e_tile = sb(tg + "_etile", [128, 64])
    arow = sb(tg + "_arow", [1, 64])
    erow = sb(tg + "_erow", [1, 64])
    acrow8 = [sb(tg + f"_acr{i}", [1, 2048]) for i in range(2)]
    D_tile = sb(tg + "_dtile", [128, 64])
    D_row = sb(tg + "_drow", [1, 64])
    convw = sb(tg + "_convw", [128, HS_T, 4])
    convb = sb(tg + "_convb", [128, HS_T])
    S = sb(tg + "_S", [128, 4096])
    S_bf = sb(tg + "_Sbf", [128, 4096], BF16)
    sil_half = sb(tg + "_sil", [128, 2048])
    B_tm = sb(tg + "_btm", [128, 256], BF16)
    trim_sb = sb(tg + "_trim", [128, 512], BF16)
    tri0f_sb = sb(tg + "_tri0f", [128, 128])
    onesf_sb = sb(tg + "_onesf", [128, 128])
    nw_sb = sb(tg + "_nw", [128, 32])
    vc0 = sb(tg + "_vc0", [128, 1])
    vc1 = sb(tg + "_vc1", [128, 1])

    ps_tra, ps_trb = env["ps_tra"], env["ps_trb"]
    ps_fra, ps_frb = env["ps_fra"], env["ps_frb"]
    ps_a, ps_b, ps_c, ps_d = env["ps_a"], env["ps_b"], env["ps_c"], env["ps_d"]

    sc = env["s_const"]
    dma(rec, "sync", trim_sb[:, 0:256], cst["trim0"][:], sc)
    dma(rec, "sync", trim_sb[:, 256:512], cst["trim1"][:], sc)
    dma(rec, "sync", tri0f_sb[:], cst["tri0f"][:], sc)
    dma(rec, "sync", onesf_sb[:], cst["onesf"][:], sc)
    dma(rec, "sync", acol[:], a_log.rearrange("(h one) -> h one", one=1), sc)
    dma(rec, "sync", dtbcol[:], dt_bias.rearrange("(h one) -> h one", one=1), sc)
    dma(rec, "sync", D_row[:], dvec.rearrange("(one h) -> one h", one=1), sc)
    dma(rec, "sync", convw[:], conv_w.rearrange("(c p) k -> p c k", p=128), sc)
    dma(rec, "sync", convb[:], conv_b.rearrange("(c p) -> p c", p=128), sc,
        allow_slow_non_contiguous=True)
    cst_tok = dma(rec, "sync", nw_sb[:], norm_w.rearrange("(c p) -> p c", p=128), sc,
                  allow_slow_non_contiguous=True)
    rec.op("scalar", lambda e: e.activation(acol[:], acol[:], AF.Exp),
           wait=[cst_tok], flush=af)
    rec.op("scalar", lambda e: e.mul(acol[:], acol[:], -1.0), flush=af, inc=(ac, 1))
    acol_done = ac.v
    rec.op("tensor", lambda e: e.matmul(ps_c[:, 0:64], env["ones_sb"][:], D_row[:],
                                        start=True, stop=True),
           wait=[cst_tok], inc=(pe, 1))
    rec.op("vector", lambda e: e.tensor_copy(D_tile[:], ps_c[:, 0:64]),
           wait=[(pe, pe.v)], inc=(ve, 1), flush=env["vf"])
    rec.op("vector", lambda e: e.memset(S[:], 0.0))
    rec.op("vector", lambda e: e.memset(S_bf[:], 0.0), inc=(ve, 1))
    init_ve = ve.v
    env[(tg, "y_evac_ve", 0)] = init_ve

    NBLK = nt // 16
    phA_store = {}
    hs_pe = 0
    for tb in range(NBLK):
        for tt in range(16):
            t = tb * 16 + tt
            wt = []
            if t >= 2:
                wt.append((ve, env.get((tg, "xa_rel", t - 2), 0)))
            xa_tok = dma(rec, "sync", xa[t % 2][:], x[t * 128:(t + 1) * 128, :],
                         st_xa, wait=wt)
            wl = [xa_tok]
            if env.get((tg, "xln_pe")):
                wl.append((pe, env[(tg, "xln_pe")]))
            ln_done = emit_ln(rec, env, xa[t % 2][:], env["trash"][:, 0:D], wl,
                              norm_extra_wait=([(pe, env.get((tg, "xln_pe"), 0))]
                                               if env.get((tg, "xln_pe")) else ()))
            env[(tg, "xa_rel", t)] = ve.v
            trv, trpe = emit_transpose_pack(
                rec, env, src_ap_fn=lambda j: env["trash"][:, j * 128:(j + 1) * 128],
                nblk=KT,
                dst_ap_fn=lambda b, nb, tt=tt: xlnT_blk[:, b * 8:b * 8 + nb,
                                                        tt * 128:(tt + 1) * 128],
                bf=True,
                wait_first=[(ve, ln_done), (pe, env.get((tg, "blk_mm_pe"), 0))])
            env[(tg, "xln_pe")] = trpe
        xlnT_ve = ve.v
        for fc in range(67):
            fw = 128 if fc < 66 else 64
            wwait = [(pe, env.get((tg, "wfc_pe", fc % 2), 0))]
            w_tok = dma(rec, "gpsimd", wfc[fc % 2][:, :, 0:fw],
                        w_in[:, fc * 128:fc * 128 + fw].rearrange("(kc p) f -> p kc f", p=128),
                        st_wf, wait=wwait)
            for ts4 in range(4):
                psm = [ps_a, ps_b][ts4 % 2]
                mm_w = [w_tok, (ve, xlnT_ve),
                        (ve, env.get((tg, "fme_ve", ts4 % 2), 0))]
                for kc in range(KT):
                    rec.op("tensor", lambda e, fc=fc, fw=fw, ts4=ts4, kc=kc, psm=psm: e.matmul(
                        psm[0:fw, :],
                        wfc[fc % 2][:, kc, 0:fw],
                        xlnT_blk[:, kc, ts4 * 512:(ts4 + 1) * 512],
                        start=(kc == 0), stop=(kc == KT - 1)),
                        wait=(mm_w if kc == 0 else ()),
                        inc=((pe, 1) if kc == KT - 1 else None))
                env[(tg, "wfc_pe", fc % 2)] = pe.v
                env[(tg, "blk_mm_pe")] = pe.v
                mm_done = pe.v
                tok0 = tb * 2048 + ts4 * 512
                if fc < 66:
                    ew = [(pe, mm_done)]
                    if (tg, "fme_so", ts4 % 2) in env:
                        ew.append(env[(tg, "fme_so", ts4 % 2)])
                    rec.op("vector", lambda e, fw=fw, ts4=ts4, psm=psm: e.tensor_copy(
                        fme[ts4 % 2][0:fw, :], psm[0:fw, :]), wait=ew, inc=(ve, 1))
                    env[(tg, "fme_ve", ts4 % 2)] = ve.v
                    if fc < 32:
                        dst = s_gateT[fc, :, tok0:tok0 + 512]
                    else:
                        dst = s_hsT[fc - 32, :, tok0:tok0 + 512]
                    env[(tg, "fme_so", ts4 % 2)] = dma(
                        rec, "sync", dst, fme[ts4 % 2][:], st_fs, wait=[(ve, ve.v)])
                else:
                    ew = [(pe, mm_done)]
                    if (tg, "dte_so") in env:
                        ew.append(env[(tg, "dte_so")])
                    rec.op("vector", lambda e, ts4=ts4, psm=psm: e.tensor_copy(
                        dte[:], psm[0:64, :]), wait=ew, inc=(ve, 1))
                    env[(tg, "fme_ve", ts4 % 2)] = ve.v
                    env[(tg, "dte_so")] = dma(rec, "sync", s_dt[:, tok0:tok0 + 512],
                                              dte[:], st_ds, wait=[(ve, ve.v)])
    phA_pe = pe.v
    phA_tok = [(c, c.v) for c in st_fs.cnts] + [(st_ds.cnts[0], st_ds.cnts[0].v)]

    if phases == "a":
        for c_, v_ in phA_tok:
            rec.wait("sync", c_, v_)
        return
    nchunks = nt * 128 // CHUNK if scan_chunks is None else scan_chunks
    for c in range(nchunks):
        t0 = c * CHUNK
        ck = (tg, "ck")
        hw = (phA_tok + [(pe, phA_pe)]) if c == 0 else \
             [env[(tg, "ynT_so")], (ve, env[(tg, "conv_ve")])]
        fsp = [(0, 9), (9, 9), (18, 8), (26, 8)]
        if c == 0:
            rec.op("vector", lambda e: e.memset(hs_sb[:, :, 0:3], 0.0),
                   wait=[(pe, phA_pe)], inc=(ve, 1), flush=vf)
            hw.append((ve, ve.v))
            for (f0, fn_) in fsp:
                hs_tok = dma(rec, "sync", hs_sb[:, f0:f0 + fn_, 3:259],
                             s_hsT[f0:f0 + fn_, :, 0:256].rearrange("c p t -> p c t"),
                             st_hs, wait=hw)
                hw = []
        else:
            for (f0, fn_) in fsp:
                hs_tok = dma(rec, "sync", hs_sb[:, f0:f0 + fn_, 0:259],
                             s_hsT[f0:f0 + fn_, :, t0 - 3:t0 + 256].rearrange("c p t -> p c t"),
                             st_hs, wait=hw)
                hw = []
        dt_tok = dma(rec, "sync", dtc[:], s_dt[:, t0:t0 + 256], st_dt,
                     wait=[(ac, env.get((tg, "dt_ac"), 0)),
                           (ve, env.get((tg, "adt_prev"), 0))] +
                          ([phA_tok[-1]] if c == 0 else []))

        cw = [hs_tok, (ve, env.get((tg, "yn_ve"), 0)),
              (ac, env.get((tg, "rms_ac"), 0)),
              (ac, env.get((tg, "silu_ac"), 0))]
        for j in range(HS_T):
            rec.op("vector", lambda e, j=j: e.tensor_scalar(
                conv_acc[:, j, :], hs_sb[:, j, 0:256],
                convw[:, j, 0:1], None, ALU.mult),
                wait=(cw if j == 0 else ()), flush=vf)
            for k in range(1, 4):
                rec.op("vector", lambda e, j=j, k=k: e.scalar_tensor_tensor(
                    conv_acc[:, j, :], hs_sb[:, j, k:k + 256],
                    convw[:, j, k:k + 1],
                    conv_acc[:, j, :], ALU.mult, ALU.add),
                    inc=((ve, 1) if (j == HS_T - 1 and k == 3) else None),
                    flush=(None if (j == HS_T - 1 and k == 3) else vf))
        env[(tg, "conv_ve")] = ve.v
        conv_ve = ve.v
        sw = [(ve, conv_ve), (pe, env.get((tg, "xtr_pe"), 0)),
              (po, env.get((tg, "scm_cons_po"), 0))]
        for j in range(HS_T):
            rec.op("scalar", lambda e, j=j: e.activation(
                xsbc[:, j, :], conv_acc[:, j, :], AF.Silu,
                bias=convb[:, j:j + 1]),
                wait=(sw if j == 0 else ()),
                inc=((ac, 1) if j == HS_T - 1 else None))
        env[(tg, "silu_ac")] = ac.v
        silu_ac = ac.v

        if upto < 2:
            continue
        xtr_w = [(ac, silu_ac), (ve, env.get((tg, "xdt_ve"), 0))]
        for st in range(2):
            trv, trpe = emit_transpose_pack(
                rec, env,
                src_ap_fn=lambda j, st=st: xsbc[:, j, st * 128:(st + 1) * 128],
                nblk=32,
                dst_ap_fn=lambda b, nb, st=st: X_tm[st][:, b * 1024:b * 1024 + nb * 128]
                    .rearrange("p (j c) -> p j c", j=nb),
                bf=True, wait_first=xtr_w)
        for st in range(2):
            rec.op("tensor", lambda e, st=st: e.transpose(
                ps_tra[:, st * 128:(st + 1) * 128].rearrange("p (a) -> p a") if False else
                ps_tra[:, st * 128:(st + 1) * 128],
                B_fm[st * 128:(st + 1) * 128] if False else xsbc[:, 32, st * 128:(st + 1) * 128],
                env["ident_sb"][:]),
                wait=([(ve, env["tr_bank_ve"].get(("bf", 0), 0))] if st == 0 else ()),
                inc=((pe, 1) if st == 1 else None))
        rec.op("vector", lambda e: e.tensor_copy(B_tm[:], ps_tra[:, 0:256]),
               wait=[(pe, pe.v)], inc=(ve, 1), flush=vf)
        env["tr_bank_ve"][("bf", 0)] = ve.v
        env[(tg, "xtr_pe")] = pe.v
        btm_ve = ve.v

        if upto < 3:
            continue
        dw_ = [dt_tok, (ac, acol_done), (ve, env.get((tg, "spf_cons_ve"), 0)),
               (pe, env.get((tg, "dt_tr_pe"), 0))]
        rec.op("scalar", lambda e: e.activation(spf[:], dtc[:], AF.Abs, bias=dtbcol[:]),
               wait=dw_, flush=af)
        rec.op("scalar", lambda e: e.activation(spf[:], spf[:], AF.Exp, scale=-1.0),
               flush=af)
        rec.op("scalar", lambda e: e.activation(spf[:], spf[:], AF.Ln, bias=1.0),
               flush=af, inc=(ac, 1))
        env[(tg, "dt_ac")] = ac.v
        sp_ln_ac = ac.v
        rec.op("vector", lambda e: e.tensor_scalar(decf[:], dtc[:], dtbcol[:], 0.0,
                                                   ALU.add, ALU.max), flush=vf)
        rec.op("vector", lambda e: e.tensor_add(spf[:], spf[:], decf[:]),
               wait=[(ac, sp_ln_ac)], flush=vf)
        if V == "l1":
            rec.op("vector", lambda e: e.tensor_scalar(spf[:], spf[:], 100.0, None,
                                                       ALU.min), flush=vf)
        rec.op("vector", lambda e: e.tensor_scalar(ddecf[:], spf[:], acol[:], None,
                                                   ALU.mult), flush=vf, inc=(ve, 1))
        adt_ve = ve.v
        env[(tg, "adt_prev")] = adt_ve
        for st in range(2):
            rec.op("tensor", lambda e, st=st: e.transpose(
                ps_fra[:, st * 64:(st + 1) * 64],
                ddecf[:, st * 128:(st + 1) * 128],
                env["identf_sb"][0:64, 0:64]),
                wait=([(ve, adt_ve), (ve, env["tr_bank_ve"].get(("fr", 0), 0))]
                      if st == 0 else ()),
                inc=((pe, 1) if st == 1 else None))
        rec.op("vector", lambda e: e.tensor_copy(adt_tm[:], ps_fra[:, 0:128]),
               wait=[(pe, pe.v)], inc=(ve, 1), flush=vf)
        env["tr_bank_ve"][("fr", 0)] = ve.v
        adttm_ve = ve.v
        aw = [(ve, adttm_ve), (ve, env.get((tg, "tsb_ve", 0), 0)),
              (ve, env.get((tg, "tsb_ve", 1), 0)),
              (ac, env.get((tg, "ehb_ac", 0), 0)),
              (ac, env.get((tg, "ehb_ac", 1), 0)),
              (ac, env.get((tg, "exn_ac", 0), 0)),
              (ac, env.get((tg, "exn_ac", 1), 0))]
        rec.op("tensor", lambda e: e.matmul(ps_b[:, 0:64], tri0f_sb[:],
                                            adt_tm[:, 0:64], start=True, stop=True),
               wait=aw)
        rec.op("tensor", lambda e: e.matmul(ps_b[:, 64:128], onesf_sb[:],
                                            adt_tm[:, 0:64], start=True, stop=False))
        rec.op("tensor", lambda e: e.matmul(ps_b[:, 64:128], tri0f_sb[:],
                                            adt_tm[:, 64:128], start=False, stop=True),
               inc=(pe, 1))
        acum_pe = pe.v
        rec.op("vector", lambda e: e.tensor_copy(acum_tm[:], ps_b[:, 0:128]),
               wait=[(pe, acum_pe)], flush=vf)
        rec.op("vector", lambda e: e.tensor_scalar(nacum_tm[:], acum_tm[:], -1.0, None,
                                                   ALU.mult), inc=(ve, 1))
        acumtm_ve = ve.v
        for lt in range(2):
            rec.op("tensor", lambda e, lt=lt: e.transpose(
                ps_frb[0:64, lt * 128:(lt + 1) * 128],
                acum_tm[:, lt * 64:(lt + 1) * 64],
                env["identf_sb"][:]),
                wait=([(ve, acumtm_ve), (ve, env["tr_bank_ve"].get(("fr", 1), 0))]
                      if lt == 0 else ()),
                inc=((pe, 1) if lt == 1 else None))
        rec.op("vector", lambda e: e.tensor_copy(acumf[:], ps_frb[0:64, 0:256]),
               wait=[(pe, pe.v)], flush=vf, inc=(ve, 1))
        env["tr_bank_ve"][("fr", 1)] = ve.v
        acumf_ve = ve.v
        rec.op("scalar", lambda e: e.activation(decf[:], acumf[:], AF.Exp, scale=-1.0,
                                                bias=acumf[:, 255:256]),
               wait=[(ve, acumf_ve)], flush=af, inc=(ac, 1))
        dec_ac = ac.v
        rec.op("vector", lambda e: e.tensor_mul(ddecf[:], spf[:], decf[:]),
               wait=[(ac, dec_ac)], inc=(ve, 1))
        env[(tg, "spf_cons_ve")] = ve.v
        ddec_ve = ve.v
        for (src, dst) in ((spf, sp_tm), (ddecf, ddec_tm)):
            for st in range(2):
                rec.op("tensor", lambda e, src=src, st=st, dst=dst: e.transpose(
                    ps_fra[:, st * 64:(st + 1) * 64],
                    src[:, st * 128:(st + 1) * 128],
                    env["identf_sb"][0:64, 0:64]),
                    wait=([(ve, ddec_ve), (ve, env["tr_bank_ve"].get(("fr", 0), 0))]
                          if st == 0 else ()),
                    inc=((pe, 1) if st == 1 else None))
            rec.op("vector", lambda e, dst=dst: e.tensor_copy(dst[:], ps_fra[:, 0:128]),
                   wait=[(pe, pe.v)], inc=(ve, 1), flush=vf)
            env["tr_bank_ve"][("fr", 0)] = ve.v
        env[(tg, "dt_tr_pe")] = pe.v
        sptm_ve = ve.v
        rec.op("tensor", lambda e: e.transpose(
            ps_frb[0:1, 0:64], acumf[:, 255:256], env["identf_sb"][0:64, 0:64]),
            wait=[(ve, acumf_ve), (ve, env["tr_bank_ve"].get(("fr", 1), 0)),
                  (ve, env.get((tg, "erow_ve"), 0))],
            inc=(pe, 1))
        rec.op("vector", lambda e: e.tensor_copy(arow[:], ps_frb[0:1, 0:64]),
               wait=[(pe, pe.v)], inc=(ve, 1), flush=vf)
        env["tr_bank_ve"][("fr", 1)] = ve.v
        rec.op("scalar", lambda e: e.activation(erow[:], arow[:], AF.Exp),
               wait=[(ve, ve.v)], inc=(ac, 1))
        erow_ac = ac.v
        rec.op("tensor", lambda e: e.matmul(ps_c[:, 0:64], env["ones_sb"][:], erow[:],
                                            start=True, stop=True),
               wait=[(ac, erow_ac), (ve, env.get((tg, "y_evac_ve", 0), 0))],
           inc=(pe, 1))
        rec.op("vector", lambda e: e.tensor_copy(e_tile[:], ps_c[:, 0:64]),
               wait=[(pe, pe.v)], inc=(ve, 1), flush=vf)
        env[(tg, "erow_ve")] = ve.v
        etile_ve = ve.v
        sacum_tok = dma(rec, "sync", s_acum[:], acumf[:], st_as,
                        wait=[(ve, acumf_ve)] +
                             ([env[(tg, "acr_ld_last")]] if (tg, "acr_ld_last") in env
                              else []))

        if upto < 4:
            continue
        for st in range(2):
            rec.op("vector", lambda e, st=st: e.tensor_tensor(
                xdt_tm[st][:].rearrange("p (h d) -> p h d", h=64),
                X_tm[st][:].rearrange("p (h d) -> p h d", h=64),
                sp_tm[:, st * 64:(st + 1) * 64].rearrange("p (h one) -> p h one", one=1)
                    .broadcast_to([128, 64, 64]),
                ALU.mult),
                wait=[(ve, sptm_ve), (pe, env.get((tg, "ydiag_pe"), 0)),
                      (ac, env.get((tg, "gate_ac"), 0))])
            rec.op("vector", lambda e, st=st: e.tensor_tensor(
                Xdec_tm[st][:].rearrange("p (h d) -> p h d", h=64),
                X_tm[st][:].rearrange("p (h d) -> p h d", h=64),
                ddec_tm[:, st * 64:(st + 1) * 64].rearrange("p (h one) -> p h one", one=1)
                    .broadcast_to([128, 64, 64]),
                ALU.mult), inc=(ve, 1))
        env[(tg, "xdt_ve")] = ve.v
        xdt_ve = ve.v

        if upto < 5:
            continue
        scw = [(ac, silu_ac), (po, env.get((tg, "scm_cons_po"), 0))]
        rec.op("tensor", lambda e: e.matmul(ps_a[:, 0:256], xsbc[:, 32, 0:128],
                                            xsbc[:, 33, :], start=True, stop=True),
               wait=scw, inc=(pe, 1))
        sc0_pe = pe.v
        rec.op("tensor", lambda e: e.matmul(ps_d[:, 0:256], xsbc[:, 32, 128:256],
                                            xsbc[:, 33, :], start=True, stop=True),
               inc=(pe, 1))
        sc1_pe = pe.v
        rec.op("vector", lambda e: e.tensor_mul(scoresm[0][:], ps_a[:, 0:256],
                                                trim_sb[:, 0:256]),
               wait=[(pe, sc0_pe)], inc=(ve, 1), flush=vf)
        rec.op("vector", lambda e: e.tensor_mul(scoresm[1][:], ps_d[:, 0:256],
                                                trim_sb[:, 256:512]),
               wait=[(pe, sc1_pe)], inc=(ve, 1), flush=vf)
        scores_ve = ve.v

        if upto < 6:
            continue
        for sl in range(8):
            stw = [(ve, xdt_ve), (ve, btm_ve), (ve, scores_ve),
                   (ve, env.get((tg, "supd_last"), 0))]
            rec.op("tensor", lambda e, sl=sl: e.matmul(
                ps_d[:], B_tm[:, 0:128], Xdec_tm[0][:, sl * 512:(sl + 1) * 512],
                start=True, stop=False), wait=stw)
            rec.op("tensor", lambda e, sl=sl: e.matmul(
                ps_d[:], B_tm[:, 128:256], Xdec_tm[1][:, sl * 512:(sl + 1) * 512],
                start=False, stop=True), inc=(pe, 1))
            st_pe = pe.v
            rec.op("vector", lambda e, sl=sl: e.tensor_tensor(
                S[:, sl * 512:(sl + 1) * 512].rearrange("p (h d) -> p h d", h=8),
                S[:, sl * 512:(sl + 1) * 512].rearrange("p (h d) -> p h d", h=8),
                e_tile[:, sl * 8:(sl + 1) * 8].rearrange("p (h one) -> p h one", one=1)
                    .broadcast_to([128, 8, 64]),
                ALU.mult),
                wait=[(pe, st_pe), (ve, etile_ve)], flush=vf)
            rec.op("vector", lambda e, sl=sl: e.tensor_add(
                S[:, sl * 512:(sl + 1) * 512], S[:, sl * 512:(sl + 1) * 512],
                ps_d[:]), flush=vf, inc=(ve, 1))
            env[(tg, "supd_last")] = ve.v
        supd_ve = ve.v

        if upto < 7:
            continue
        y_bank = [ps_c, ps_a]
        for h in range(nheads):
            g8 = h // 8
            par = h % 2
            if h % 8 == 0:
                aw8 = [sacum_tok, (pe, env.get((tg, "t1_pe", g8 % 2), 0))]
                acr_tok = dma(rec, "sync", acrow8[g8 % 2][:],
                              s_acum.rearrange("h l -> (h l)").rearrange(
                                  "(one f) -> one f", one=1)[0:1, g8 * 2048:(g8 + 1) * 2048],
                              st_ar, wait=aw8)
                env[(tg, "acr_ld", g8 % 2)] = acr_tok
                env[(tg, "acr_ld_last")] = acr_tok
            if h % 2 == 0:
                t1w = [env[(tg, "acr_ld", g8 % 2)],
                       (ve, env.get((tg, "tsb_ve", 1), 0)),
                       (ac, env.get((tg, "exn_ac", 1), 0))]
                if h == 0:
                    t1w.append((ve, acumtm_ve))
                rec.op("tensor", lambda e, h=h, g8=g8: e.matmul(
                    ps_b[:], env["ones_sb"][:],
                    acrow8[g8 % 2][0:1, (h % 8) * 256:(h % 8) * 256 + 512],
                    start=True, stop=True),
                    wait=t1w, inc=(pe, 1))
                env[(tg, "t1_pe", g8 % 2)] = pe.v
                t1_pe = pe.v
            half = (h % 2) * 256
            for st in range(2):
                rec.op("vector", lambda e, st=st, half=half, h=h, par=par: e.tensor_scalar(
                    tsb[par][:, st * 256:(st + 1) * 256], ps_b[:, half:half + 256],
                    nacum_tm[:, st * 64 + h:st * 64 + h + 1], 0.0,
                    ALU.add, ALU.min),
                    wait=([(pe, t1_pe), (ve, acumtm_ve),
                           (ac, env.get((tg, "ehb_ac", par), 0))] if st == 0 else ()),
                    inc=((ve, 1) if st == 1 else None))
            env[(tg, "tsb_ve", par)] = ve.v
            rec.op("scalar", lambda e, par=par: e.activation(
                ehb[par][:], tsb[par][:], AF.Exp),
                wait=[(ve, ve.v), (ve, env.get((tg, "mh_ve", par), 0)),
                      (po, env.get((tg, "mh_po", par), 0))],
                inc=(ac, 1))
            env[(tg, "ehb_ac", par)] = ac.v
            ehb_ac = ac.v
            rec.op("vector", lambda e, par=par: e.tensor_mul(
                Mh[par][0][:], ehb[par][:, 0:256], scoresm[0][:]),
                wait=[(ac, ehb_ac), (pe, env.get((tg, "ymm_pe", par), 0))],
                inc=(ve, 1))
            env[(tg, "mh_ve", par)] = ve.v
            rec.op("gpsimd", lambda e, par=par: e.tensor_mul(
                Mh[par][1][:], ehb[par][:, 256:512], scoresm[1][:]),
                wait=[(ac, ehb_ac), (pe, env.get((tg, "ymm_pe", par), 0))],
                inc=(po, 1))
            env[(tg, "mh_po", par)] = po.v
            mh_po = po.v
            mh_ve = env[(tg, "mh_ve", par)]
            rec.op("scalar", lambda e, half=half, par=par: e.activation(
                exn[par][:], ps_b[:, half:half + 256], AF.Exp),
                wait=[(pe, t1_pe), (po, env.get((tg, "chp_po", par), 0))], inc=(ac, 1))
            env[(tg, "exn_ac", par)] = ac.v
            rec.op("gpsimd", lambda e, par=par: e.tensor_mul(
                chp[par][:], xsbc[:, 33, :], exn[par][:]),
                wait=[(ac, ac.v), (pe, env.get((tg, "ymm_pe", par), 0))], inc=(po, 1))
            env[(tg, "chp_po", par)] = po.v
            chp_po = po.v
            yb = y_bank[par]
            yw0 = [(po, mh_po), (ve, mh_ve), (po, chp_po),
                   (ve, env.get((tg, "y_evac_ve", par), 0))]
            if h == 0:
                yw0.append((ve, etile_ve))
            if h == 1:
                yw0.append((ve, scores_ve))
            for lt in range(2):
                rec.op("tensor", lambda e, lt=lt, h=h, yb=yb, par=par: e.matmul(
                    yb[:, lt * 64:(lt + 1) * 64], Mh[par][0][:, lt * 128:(lt + 1) * 128],
                    xdt_tm[0][:, h * 64:(h + 1) * 64], start=True, stop=False),
                    wait=(yw0 if lt == 0 else ()))
                rec.op("tensor", lambda e, lt=lt, h=h, yb=yb, par=par: e.matmul(
                    yb[:, lt * 64:(lt + 1) * 64], Mh[par][1][:, lt * 128:(lt + 1) * 128],
                    xdt_tm[1][:, h * 64:(h + 1) * 64], start=False, stop=False))
                rec.op("tensor", lambda e, lt=lt, h=h, yb=yb, par=par: e.matmul(
                    yb[:, lt * 64:(lt + 1) * 64], chp[par][:, lt * 128:(lt + 1) * 128],
                    S_bf[:, h * 64:(h + 1) * 64], start=False, stop=True),
                    inc=((pe, 1) if lt == 1 else None))
            env[(tg, "ymm_pe", par)] = pe.v
            env[(tg, "ydiag_pe")] = pe.v
            ymm_pe = pe.v
            for lt in range(2):
                rec.op("vector", lambda e, lt=lt, h=h, yb=yb: e.scalar_tensor_tensor(
                    y_tm[lt][:, h * 64:(h + 1) * 64],
                    X_tm[lt][:, h * 64:(h + 1) * 64],
                    D_tile[:, h:h + 1],
                    yb[:, lt * 64:(lt + 1) * 64],
                    ALU.mult, ALU.add),
                    wait=([(pe, ymm_pe), (ve, env.get((tg, "yn_ve"), 0))]
                          if lt == 0 else ()),
                    inc=((ve, 1) if lt == 1 else None))
            env[(tg, "y_evac_ve", par)] = ve.v
        heads_ve = ve.v
        heads_pe = pe.v
        heads_po = po.v
        env[(tg, "scm_cons_po")] = po.v
        if not do_post:
            continue

        rec.op("vector", lambda e: e.tensor_copy(S_bf[:], S[:]),
               wait=[(pe, heads_pe)], inc=(ve, 1))
        sbf_ve = ve.v

        gw_ = [(pe, heads_pe)]
        if (tg, "ynT_so") in env:
            gw_.append(env[(tg, "ynT_so")])
        for q in range(4):
            gate_tok = dma(rec, "sync", gate_fm[:, q * 8:(q + 1) * 8, :],
                           s_gateT[q * 8:(q + 1) * 8, :, t0:t0 + 256].rearrange("c p t -> p c t"),
                           st_gt, wait=(gw_ if q == 0 else ()))
        for st in range(2):
            trv, trpe = emit_transpose_pack(
                rec, env,
                src_ap_fn=lambda j, st=st: gate_fm[:, j, st * 128:(st + 1) * 128],
                nblk=32,
                dst_ap_fn=lambda b, nb, st=st: gate_tm[st][:, b * 1024:b * 1024 + nb * 128]
                    .rearrange("p (j c) -> p j c", j=nb),
                bf=True, wait_first=[gate_tok, (ve, supd_ve), (pe, heads_pe)])
        gate_tr_ve = trv
        gate_tr_pe = trpe
        env[(tg, "xtr_pe")] = max(env.get((tg, "xtr_pe"), 0), gate_tr_pe)
        for lt in range(2):
            if V == "l0":
                for sub in range(2):
                    rec.op("scalar", lambda e, lt=lt, sub=sub: e.activation(
                        sil_half[:], gate_tm[lt][:, sub * 2048:(sub + 1) * 2048],
                        AF.Silu),
                        wait=[(ve, gate_tr_ve), (ve, env.get((tg, "sil_cons_ve"), 0))],
                        inc=(ac, 1))
                    rec.op("vector", lambda e, lt=lt, sub=sub: e.tensor_mul(
                        y_tm[lt][:, sub * 2048:(sub + 1) * 2048],
                        y_tm[lt][:, sub * 2048:(sub + 1) * 2048],
                        sil_half[:]),
                        wait=[(ac, ac.v), (ve, heads_ve)], flush=vf, inc=(ve, 1))
                    env[(tg, "sil_cons_ve")] = ve.v
                yg_ve = ve.v
                rec.op("scalar", lambda e, lt=lt: e.activation(
                    env["trash"][:, 0:2048], y_tm[lt][:, 0:2048], AF.Square,
                    accum_out=vc0[:]),
                    wait=[(ve, yg_ve), (ve, env.get((tg, "vc_cons_ve"), 0))], flush=af)
                rec.op("scalar", lambda e, lt=lt: e.activation(
                    env["trash"][:, 0:2048], y_tm[lt][:, 2048:4096], AF.Square,
                    accum_out=vc1[:]), flush=af, inc=(ac, 1))
                sq_ac = ac.v
                rec.op("vector", lambda e: e.tensor_add(vc0[:], vc0[:], vc1[:]),
                       wait=[(ac, sq_ac)], flush=vf)
                rec.op("vector", lambda e: e.tensor_scalar(vc0[:], vc0[:], 1.0 / INTER,
                                                           None, ALU.mult),
                       flush=vf, inc=(ve, 1))
                rec.op("scalar", lambda e: e.activation(vc1[:], vc0[:], AF.Ln,
                                                        bias=env["eps_col"][:]),
                       wait=[(ve, ve.v)], flush=af)
                rec.op("scalar", lambda e: e.activation(vc1[:], vc1[:], AF.Exp,
                                                        scale=-0.5),
                       flush=af, inc=(ac, 1))
                rstd_ac = ac.v
                env[(tg, "rms_ac")] = rstd_ac
                env[(tg, "gate_ac")] = max(env.get((tg, "gate_ac"), 0), ac.v)
                rec.op("vector", lambda e, lt=lt: e.tensor_scalar(
                    X_tm[lt][:], y_tm[lt][:], vc1[:], None, ALU.mult),
                    wait=[(ac, rstd_ac)], inc=(ve, 1))
                env[(tg, "vc_cons_ve")] = ve.v
            else:
                rec.op("scalar", lambda e, lt=lt: e.activation(
                    env["trash"][:, 0:2048], y_tm[lt][:, 0:2048], AF.Square,
                    accum_out=vc0[:]),
                    wait=[(ve, heads_ve), (ve, env.get((tg, "vc_cons_ve"), 0))], flush=af)
                rec.op("scalar", lambda e, lt=lt: e.activation(
                    env["trash"][:, 0:2048], y_tm[lt][:, 2048:4096], AF.Square,
                    accum_out=vc1[:]), flush=af, inc=(ac, 1))
                sq_ac = ac.v
                rec.op("vector", lambda e: e.tensor_add(vc0[:], vc0[:], vc1[:]),
                       wait=[(ac, sq_ac)], flush=vf)
                rec.op("vector", lambda e: e.tensor_scalar(vc0[:], vc0[:], 1.0 / INTER,
                                                           None, ALU.mult),
                       flush=vf, inc=(ve, 1))
                rec.op("scalar", lambda e: e.activation(vc1[:], vc0[:], AF.Ln,
                                                        bias=env["eps_col"][:]),
                       wait=[(ve, ve.v)], flush=af)
                rec.op("scalar", lambda e: e.activation(vc1[:], vc1[:], AF.Exp,
                                                        scale=-0.5),
                       flush=af, inc=(ac, 1))
                rstd_ac = ac.v
                for sub in range(2):
                    rec.op("scalar", lambda e, lt=lt, sub=sub: e.activation(
                        sil_half[:], gate_tm[lt][:, sub * 2048:(sub + 1) * 2048],
                        AF.Silu),
                        wait=[(ve, gate_tr_ve), (ve, env.get((tg, "sil_cons_ve"), 0)),
                              (ac, rstd_ac)],
                        inc=(ac, 1))
                    rec.op("vector", lambda e, lt=lt, sub=sub: e.scalar_tensor_tensor(
                        X_tm[lt][:, sub * 2048:(sub + 1) * 2048],
                        sil_half[:], vc1[:],
                        y_tm[lt][:, sub * 2048:(sub + 1) * 2048],
                        ALU.mult, ALU.mult),
                        wait=[(ac, ac.v)], inc=(ve, 1))
                    env[(tg, "sil_cons_ve")] = ve.v
                    env[(tg, "vc_cons_ve")] = ve.v
                env[(tg, "rms_ac")] = ac.v
                env[(tg, "gate_ac")] = max(env.get((tg, "gate_ac"), 0), ac.v)
        yn_ve = ve.v
        env[(tg, "yn_ve")] = yn_ve
        ytw = [(ve, yn_ve)]
        if (tg, "ynT_so") in env:
            ytw.append(env[(tg, "ynT_so")])
        for lt in range(2):
            trv, trpe = emit_transpose_pack(
                rec, env,
                src_ap_fn=lambda j, lt=lt: X_tm[lt][:, j * 128:(j + 1) * 128],
                nblk=32,
                dst_ap_fn=lambda b, nb, lt=lt: ynT[:, b * 8:b * 8 + nb,
                                                   lt * 128:(lt + 1) * 128],
                bf=True, wait_first=ytw)
        for q in range(4):
            yn_tok = dma(rec, "sync",
                         s_ynT[q * 8:(q + 1) * 8, :, t0:t0 + 256].rearrange("c p t -> p c t"),
                         ynT[:, q * 8:(q + 1) * 8, :], st_ys,
                         wait=([(ve, trv)] if q == 0 else ()))
        env[(tg, "ynT_so")] = yn_tok
    scan_tok = env.get((tg, "ynT_so"))
    scan_pe = pe.v
    scan_ve = ve.v
    if phases == "ab":
        if scan_tok:
            rec.wait("sync", scan_tok[0], scan_tok[1])
        for c_, v_ in phA_tok:
            rec.wait("sync", c_, v_)
        rec.wait("vector", ve, ve.v)
        return

    for ns in range(4):
        for kb in range(32):
            wlw = [(ve, env.get((tg, "wkf_ve", kb % 2), 0))]
            if ns == 0 and kb < 2:
                wlw.append((ve, scan_ve))
            wk_tok = dma(rec, "sync", wkf[kb % 2][:],
                         w_out[kb * 128:(kb + 1) * 128, ns * 512:(ns + 1) * 512],
                         st_wk, wait=wlw)
            wtw = [wk_tok, (pe, env.get((tg, "wns_pe"), 0))]
            if ns == 0 and kb == 0:
                wtw += ([scan_tok] if scan_tok else []) + \
                       [(ve, scan_ve), (ac, env.get((tg, "rms_ac"), 0))]
            rec.op("vector", lambda e, kb=kb, ns=ns: e.tensor_scalar(
                wns[:, kb, :], wkf[kb % 2][:], nw_sb[:, kb:kb + 1], None, ALU.mult),
                wait=wtw, inc=(ve, 1))
            env[(tg, "wkf_ve", kb % 2)] = ve.v
        wns_ve = ve.v
        for tt in range(nt):
            gidx = ns * nt + tt
            t0 = tt * 128
            ytw = ([scan_tok] if scan_tok else []) if gidx < 2 else \
                  [(pe, env.get((tg, "ynTt_pe", gidx % 2), 0))]
            for q in range(4):
                yt_tok = dma(rec, "sync", ynTt[gidx % 2][:, q * 8:(q + 1) * 8, :],
                             s_ynT[q * 8:(q + 1) * 8, :, t0:t0 + 128].rearrange("c p t -> p c t"),
                             st_yt, wait=(ytw if q == 0 else ()))
            xw_ = [(ve, env.get((tg, "xcs_ve", gidx % 2), 0))]
            if gidx < 2:
                xw_.append((ve, scan_ve))
            xc_tok = dma(rec, "sync", xcs[gidx % 2][:],
                         x[t0:t0 + 128, ns * 512:(ns + 1) * 512], st_xc, wait=xw_)
            psm = [ps_a, ps_b][gidx % 2]
            mw = [yt_tok, xc_tok, (ve, wns_ve), (ve, env.get((tg, "out_ve", gidx % 2), 0))]
            for kb in range(32):
                rec.op("tensor", lambda e, kb=kb, ns=ns, gidx=gidx, psm=psm: e.matmul(
                    psm[:], ynTt[gidx % 2][:, kb, :], wns[:, kb, :],
                    start=(kb == 0), stop=(kb == 31)),
                    wait=(mw if kb == 0 else ()),
                    inc=((pe, 1) if kb == 31 else None))
            env[(tg, "ynTt_pe", gidx % 2)] = pe.v
            env[(tg, "wns_pe")] = pe.v
            omm_pe = pe.v
            ow_ = [(pe, omm_pe)]
            if (tg, "out_so", gidx % 2) in env:
                ow_.append(env[(tg, "out_so", gidx % 2)])
            rec.op("vector", lambda e, gidx=gidx, psm=psm: e.tensor_add(
                outs[gidx % 2][:], psm[:], xcs[gidx % 2][:]),
                wait=ow_, inc=(ve, 1))
            env[(tg, "xcs_ve", gidx % 2)] = ve.v
            env[(tg, "out_ve", gidx % 2)] = ve.v
            env[(tg, "out_so", gidx % 2)] = dma(
                rec, "sync", xo[t0:t0 + 128, ns * 512:(ns + 1) * 512],
                outs[gidx % 2][:], st_os, wait=[(ve, ve.v)])

    for cnt in st_os.cnts:
        rec.wait("sync", cnt, cnt.v)


def build_mamba(variant, nt=NT, dbg=False, phases="abc", scan_chunks=None, nheads=NH, do_post=True, upto=99):
    nc = bass.Bass("TRN2", target_bir_lowering=False, debug=False)
    kind_s = "ExternalOutput" if dbg else "Internal"
    aps = {
        "x": nc.dram_tensor("x", [SEQ, D], F32, kind="ExternalInput").ap(),
        "w_in": nc.dram_tensor("w_in", [D, 8512], F32, kind="ExternalInput").ap(),
        "conv_w": nc.dram_tensor("conv_w", [CDIM, 4], F32, kind="ExternalInput").ap(),
        "conv_b": nc.dram_tensor("conv_b", [CDIM], F32, kind="ExternalInput").ap(),
        "dt_bias": nc.dram_tensor("dt_bias", [64], F32, kind="ExternalInput").ap(),
        "a_log": nc.dram_tensor("a_log", [64], F32, kind="ExternalInput").ap(),
        "dvec": nc.dram_tensor("dvec", [64], F32, kind="ExternalInput").ap(),
        "norm_w": nc.dram_tensor("norm_w", [INTER], F32, kind="ExternalInput").ap(),
        "w_out": nc.dram_tensor("w_out", [INTER, D], F32, kind="ExternalInput").ap(),
        "xo": nc.dram_tensor("xo", [SEQ, D], F32, kind="ExternalOutput").ap(),
        "s_hsT": nc.dram_tensor("s_hsT", [HS_T, 128, SEQ], BF16, kind=kind_s).ap(),
        "s_gateT": nc.dram_tensor("s_gateT", [GATE_T, 128, SEQ], BF16, kind=kind_s).ap(),
        "s_dt": nc.dram_tensor("s_dt", [64, SEQ], F32, kind=kind_s).ap(),
        "s_ynT": nc.dram_tensor("s_ynT", [GATE_T, 128, SEQ], BF16, kind=kind_s).ap(),
        "s_acum": nc.dram_tensor("s_acum", [64, 256], F32, kind=kind_s).ap(),
    }
    ident_bf, identf, ones_f = mk_consts(nc)
    cst = mamba_consts(nc, "mb")
    ctx = ExitStack()
    with ctx:
        env, sems, sb, psa = common_env(nc, ctx)
        rec = Rec()
        emit_load_consts(rec, env, ident_bf, identf, ones_f)
        emit_mamba(rec, env, aps, sb, psa, variant, cst, nt=nt, phases=phases, scan_chunks=scan_chunks, nheads=nheads, do_post=do_post, upto=upto)
        rec.check()
        block = ctx.enter_context(nc.Block())
        rec.replay(block)
    return nc


ARENA_BYTES = 206336


def build_fused(nt=NT):
    nc = bass.Bass("TRN2", target_bir_lowering=False, debug=False)
    dt_ = lambda n, s, k="ExternalInput", d=F32: nc.dram_tensor(n, s, d, kind=k).ap()
    xo = dt_("xo", [SEQ, D], "ExternalOutput")
    x_in = dt_("x", [SEQ, D])
    x1 = dt_("x1", [SEQ, D], "Internal")
    x2 = dt_("x2", [SEQ, D], "Internal")
    x3 = dt_("x3", [SEQ, D], "Internal")

    def mamba_aps(pref, xi, xo_):
        return {
            "x": xi, "xo": xo_,
            "w_in": dt_(pref + "w_in", [D, 8512]),
            "conv_w": dt_(pref + "conv_w", [CDIM, 4]),
            "conv_b": dt_(pref + "conv_b", [CDIM]),
            "dt_bias": dt_(pref + "dt_bias", [64]),
            "a_log": dt_(pref + "a_log", [64]),
            "dvec": dt_(pref + "dvec", [64]),
            "norm_w": dt_(pref + "norm_w", [INTER]),
            "w_out": dt_(pref + "w_out", [INTER, D]),
            "s_hsT": dt_(pref + "s_hsT", [HS_T, 128, SEQ], "Internal", BF16),
            "s_gateT": dt_(pref + "s_gateT", [GATE_T, 128, SEQ], "Internal", BF16),
            "s_dt": dt_(pref + "s_dt", [64, SEQ], "Internal"),
            "s_ynT": dt_(pref + "s_ynT", [GATE_T, 128, SEQ], "Internal", BF16),
            "s_acum": dt_(pref + "s_acum", [64, 256], "Internal"),
        }

    aps_l0 = mamba_aps("l0_", x_in, x1)
    aps_f0 = {"x": x1, "xo": x2, "w": dt_("f0_w", [D, D]), "bias": dt_("f0_bias", [D])}
    aps_l1 = mamba_aps("l1_", x2, x3)
    aps_m = {"x": x3, "xo": xo,
             "gwt": dt_("m_gwt", [D, 8]),
             "w1t": dt_("m_w1t", [NE, D, MI]), "w3t": dt_("m_w3t", [NE, D, MI]),
             "w2t": dt_("m_w2t", [NE, MI, D]),
             "sw1t": dt_("m_sw1t", [D, 2 * MI]), "sw3t": dt_("m_sw3t", [D, 2 * MI]),
             "sw2t": dt_("m_sw2t", [2 * MI, D]),
             "s_xlnT": dt_("m_s_xlnT", [NT, 128, D], "Internal", BF16),
             "which": None}

    ident_bf, identf, ones_f = mk_consts(nc)
    cst = mamba_consts(nc, "mb")
    ctx = ExitStack()
    with ctx:
        env, sems, sb, psa = common_env(nc, ctx)
        arena_h = ctx.enter_context(nc.sbuf_tensor("phase_arena", [128, ARENA_BYTES],
                                                    mybir.dt.uint8))
        pa = PhaseAlloc(arena_h.ap(), ARENA_BYTES)
        rec = Rec()
        emit_load_consts(rec, env, ident_bf, identf, ones_f)
        emit_mamba(rec, env, aps_l0, pa.sb, psa, "l0", cst, nt=nt)
        emit_barrier(rec, env)
        pa.reset()
        emit_ffn0(rec, env, aps_f0, pa.sb, psa)
        emit_barrier(rec, env)
        pa.reset()
        emit_mamba(rec, env, aps_l1, pa.sb, psa, "l1", cst, nt=nt)
        emit_barrier(rec, env)
        pa.reset()
        emit_moe(rec, env, aps_m, pa.sb, psa, nt=nt)
        emit_barrier(rec, env)
        rec.check()
        block = ctx.enter_context(nc.Block())
        rec.replay(block)
    return nc



_BUILD_CACHE = {}


def _get(name, fn):
    if name not in _BUILD_CACHE:
        _BUILD_CACHE[name] = fn()
    return _BUILD_CACHE[name]


def kernel(**inputs):
    z = inputs
    ncore = 8
    c = np.ascontiguousarray
    x = c(z["L0_attn_x"]).astype(np.float32, copy=False)
    nc = _get("fused", build_fused)
    shared = {
        "l0_w_in": z["L0_attn_in_proj_w"],
        "l0_conv_w": z["L0_attn_conv_w"],
        "l0_conv_b": z["L0_attn_conv_b"],
        "l0_dt_bias": z["L0_attn_dt_bias"],
        "l0_a_log": z["L0_attn_A_log"],
        "l0_dvec": z["L0_attn_D"],
        "l0_norm_w": z["L0_attn_norm_w"],
        "l0_w_out": z["L0_attn_out_proj_w"],
        "f0_w": z["L0_ffn_weight"],
        "f0_bias": z["L0_ffn_bias"],
        "l1_w_in": c(z["L1_attn_in_proj_w"].T),
        "l1_conv_w": c(z["L1_attn_conv_w"].reshape(CDIM, 4)),
        "l1_conv_b": z["L1_attn_conv_b"],
        "l1_dt_bias": z["L1_attn_dt_bias"],
        "l1_a_log": z["L1_attn_A_log"],
        "l1_dvec": z["L1_attn_D"],
        "l1_norm_w": z["L1_attn_norm_w"],
        "l1_w_out": c(z["L1_attn_out_proj_w"].T),
        "m_gwt": c(z["L1_ffn_gate_w"].T),
        "m_w1t": c(np.transpose(z["L1_ffn_w1"], (0, 2, 1))),
        "m_w3t": c(np.transpose(z["L1_ffn_w3"], (0, 2, 1))),
        "m_w2t": c(np.transpose(z["L1_ffn_w2"], (0, 2, 1))),
        "m_sw1t": c(z["L1_ffn_sw1"].T),
        "m_sw3t": c(z["L1_ffn_sw3"].T),
        "m_sw2t": c(z["L1_ffn_sw2"].T),
    }
    in_maps = [dict(shared, x=x[b]) for b in range(ncore)]
    res = run_bass_kernel_spmd(nc, in_maps, core_ids=list(range(ncore)))
    return np.stack([r["xo"] for r in res.results], axis=0)


# version 03 (final)
# speedup vs baseline: 1.1858x; 1.1858x over previous
import numpy as np
import ml_dtypes
import concourse.bass as bass
import concourse.mybir as mybir
from concourse.bass_utils import run_bass_kernel_spmd
from contextlib import ExitStack

F32 = mybir.dt.float32
BF16 = mybir.dt.bfloat16
AF = mybir.ActivationFunctionType
ALU = mybir.AluOpType
AX = mybir.AxisListType
BF = ml_dtypes.bfloat16

SEQ = 4096
D = 2048
NT = SEQ // 128
KT = D // 128
EPS = 1e-5

ENGINES = ("sync", "gpsimd", "scalar", "vector", "tensor")


class Cnt:
    def __init__(self, handle, name):
        self.h = handle
        self.v = 0
        self.name = name

    def inc(self, k=1):
        self.v += k
        return self.v


class Rec:

    def __init__(self):
        self.streams = {k: [] for k in ENGINES}
        self.meta = {k: [] for k in ENGINES}
        self.dma_sems = set()

    def op(self, eng, fn, wait=(), inc=None, flush=None, dma_sem=False):
        w = [(c.h, val) for (c, val) in wait if val > 0]
        wname = [(id(c.h), val, c.name) for (c, val) in wait if val > 0]
        if inc is not None:
            cnt, k = inc
            cnt.inc(k)
            tgt = (cnt.h, k)
        else:
            tgt = None
        ftgt = None
        fonly = None
        if flush is not None:
            if inc is not None:
                fonly = (inc[0].h, inc[0].v)
            else:
                flush.inc(1)
                ftgt = (flush.h, flush.v)

        def emit(e, w=w, tgt=tgt, fn=fn, ftgt=ftgt, fonly=fonly):
            for h, val in w:
                e.wait_ge(h, val)
            ins = fn(e)
            if tgt is not None:
                ins.then_inc(tgt[0], tgt[1])
            if ftgt is not None:
                ins.then_inc(ftgt[0], 1)
                e.wait_ge(ftgt[0], ftgt[1])
            if fonly is not None:
                e.wait_ge(fonly[0], fonly[1])
        self.streams[eng].append(emit)
        mw = [(id(c.h), val, c.name) for (c, val) in wait if val > 0]
        minc = []
        if inc is not None:
            minc.append((id(inc[0].h), inc[1]))
        if flush is not None and inc is None:
            minc.append((id(flush.h), 1))
        if dma_sem:
            self.dma_sems.add(id(inc[0].h))
        import traceback
        st = traceback.extract_stack(limit=8)
        fr = next((f for f in reversed(st) if "emit_" in (f.name or "")), st[0])
        self.meta[eng].append((wname, minc, f"{fr.name}:{fr.lineno}"))

    def wait(self, eng, cnt, val):
        if val > 0:
            self.streams[eng].append(lambda e, h=cnt.h, v=val: e.wait_ge(h, v))
            self.meta[eng].append(([(id(cnt.h), val)], [], "barrier"))

    def check(self):
        pos = {k: 0 for k in ENGINES}
        sems = {}
        progress = True
        total = sum(len(v) for v in self.meta.values())
        done = 0
        while progress:
            progress = False
            for eng in ENGINES:
                while pos[eng] < len(self.meta[eng]):
                    mw, minc, tag = self.meta[eng][pos[eng]]
                    if all(sems.get(h, 0) >= v for h, v, *_ in mw):
                        for h, k in minc:
                            sems[h] = sems.get(h, 0) + k
                        pos[eng] += 1
                        done += 1
                        progress = True
                    else:
                        break
        if done < total:
            lines = []
            for eng in ENGINES:
                if pos[eng] < len(self.meta[eng]):
                    mw, minc, tag = self.meta[eng][pos[eng]]
                    missing = [(n, v, sems.get(h, 0)) for h, v, *n in mw
                               if sems.get(h, 0) < v]
                    lines.append(f"  {eng} stuck at #{pos[eng]} ({tag}): needs {missing}")
            raise RuntimeError(f"DEADLOCK: {done}/{total} ops executed\n" + "\n".join(lines))
        races = self.race_check()
        if races:
            raise RuntimeError("DMA-SEM RACES:\n" + "\n".join(races[:20]))
        return True

    def race_check(self):
        pos = {k: 0 for k in ENGINES}
        sems = {}
        issued = {}
        races = []
        progress = True
        while progress:
            progress = False
            for eng in ENGINES:
                while pos[eng] < len(self.meta[eng]):
                    mw, minc, tag = self.meta[eng][pos[eng]]
                    if all(sems.get(h, 0) >= v for h, v, *_ in mw):
                        for h, v, *nm in mw:
                            if h in self.dma_sems and v < 16 * issued.get(h, 0) and v > 0:
                                races.append(
                                    f"  {eng} #{pos[eng]} ({tag}): waits {v} on dma-sem "
                                    f"{nm} with {issued.get(h, 0)} issued (needs =={16*issued.get(h,0)})")
                        for h, k in minc:
                            sems[h] = sems.get(h, 0) + k
                            if h in self.dma_sems and k == 16:
                                issued[h] = issued.get(h, 0) + 1
                        pos[eng] += 1
                        progress = True
                    else:
                        break
        return races

    def replay(self, block):
        for name in ENGINES:
            items = self.streams[name]
            if items:
                getattr(block, name)(lambda e, items=items: [f(e) for f in items])


class Stream:

    def __init__(self, env, name, nbuf=1):
        nc, ctx = env["nc"], env["ctx"]
        self.cnts = [Cnt(ctx.enter_context(nc.semaphore(f"{name}{i}")), f"{name}{i}")
                     for i in range(nbuf)]
        self.nbuf = nbuf
        self.i = 0
        self.eng = None
        env.setdefault("_streams", []).append(self)

    def nxt(self):
        c = self.cnts[self.i % self.nbuf]
        self.i += 1
        return c


def dma(rec, eng, out, in_, ld, wait=(), **kw):
    if isinstance(ld, Stream):
        c = ld.nxt()
        ld.eng = eng
    else:
        c = ld
    rec.op(eng, lambda e: e.dma_start(out=out, in_=in_, **kw), wait=wait,
           inc=(c, 16), dma_sem=True)
    return (c, c.v)



def emit_ln(rec, env, x_ap, xln_bf_ap, wait_loaded, norm_extra_wait=()):
    ve, ac = env["ve"], env["ac"]
    t = env
    vf, af = env["vf"], env["af"]
    rec.op("scalar", lambda e: e.activation(t["trash"][:], x_ap, AF.Square,
                                            accum_out=t["sumsq_c"][:]),
           wait=wait_loaded, inc=(ac, 1))
    sq_done = ac.v
    rec.op("vector", lambda e: e.reduce_sum(t["sum_c"][:], x_ap, axis=AX.X),
           wait=wait_loaded, flush=vf)
    rec.op("vector", lambda e: e.tensor_scalar(t["mean_c"][:], t["sum_c"][:], 1.0 / D, None, ALU.mult),
           flush=vf)
    rec.op("vector", lambda e: e.tensor_scalar(t["var_c"][:], t["sumsq_c"][:], 1.0 / D, None, ALU.mult),
           wait=[(ac, sq_done)], flush=vf)
    rec.op("vector", lambda e: e.tensor_mul(t["m2_c"][:], t["mean_c"][:], t["mean_c"][:]), flush=vf)
    rec.op("vector", lambda e: e.tensor_sub(t["var_c"][:], t["var_c"][:], t["m2_c"][:]), inc=(ve, 1))
    var_done = ve.v
    rec.op("scalar", lambda e: e.activation(t["lnv_c"][:], t["var_c"][:], AF.Ln, bias=t["eps_col"][:]),
           wait=[(ve, var_done)], flush=af)
    rec.op("scalar", lambda e: e.activation(t["rstd_c"][:], t["lnv_c"][:], AF.Exp, scale=-0.5),
           inc=(ac, 1))
    rstd_done = ac.v
    rec.op("vector", lambda e: e.tensor_scalar(xln_bf_ap, x_ap, t["mean_c"][:], t["rstd_c"][:],
                                               ALU.subtract, ALU.mult),
           wait=[(ac, rstd_done)] + list(norm_extra_wait), inc=(ve, 1))
    return ve.v


def emit_transpose_pack(rec, env, src_ap_fn, nblk, dst_ap_fn, bf, wait_first=()):
    ve, pe = env["ve"], env["pe"]
    ident = env["ident_sb"] if bf else env["identf_sb"]
    banks = (env["ps_tra"], env["ps_trb"]) if bf else (env["ps_fra"], env["ps_frb"])
    bkey = "bf" if bf else "fr"
    bpb = 8 if bf else 4
    nbank = (nblk + bpb - 1) // bpb
    for b in range(nbank):
        bank = banks[b % 2]
        lo, hi = b * bpb, min(b * bpb + bpb, nblk)
        nb = hi - lo
        w = list(wait_first) if b < 2 else []
        prev = env["tr_bank_ve"].get((bkey, b % 2))
        if prev:
            w.append((env["ve"], prev))
        for j in range(lo, hi):
            rec.op("tensor", lambda e, j=j, lo=lo, bank=bank: e.transpose(
                bank[:, (j - lo) * 128:(j - lo) * 128 + 128], src_ap_fn(j), ident[:]),
                wait=(w if j == lo else ()),
                inc=((pe, 1) if j == hi - 1 else None))
        pe_done = pe.v
        rec.op("vector", lambda e, bank=bank, nb=nb, b=b: e.tensor_copy(
            dst_ap_fn(b, nb),
            bank[:, 0:nb * 128].rearrange("p (j c) -> p j c", j=nb)),
            wait=[(pe, pe_done)], inc=(ve, 1), flush=env["vf"])
        env["tr_bank_ve"][(bkey, b % 2)] = ve.v
    return ve.v, pe.v


def common_env(nc, ctx):
    sb = lambda name, shape, dt=F32: ctx.enter_context(nc.sbuf_tensor(name, shape, dt))
    psa = lambda name, shape, dt=F32: ctx.enter_context(nc.psum_tensor(name, shape, dt))
    env = {"tr_bank_ve": {}}
    env["ident_sb"] = sb("ident_sb", [128, 128], BF16)
    env["identf_sb"] = sb("identf_sb", [128, 128], F32)
    env["ones_sb"] = sb("ones_sb", [1, 128])
    for n in ("sum_c", "sumsq_c", "mean_c", "var_c", "m2_c", "lnv_c", "rstd_c"):
        env[n] = sb(n, [128, 1])
    env["trash"] = sb("trash", [128, D], BF16)
    env["eps_col"] = sb("eps_col", [128, 1])
    env["ps_tra"] = psa("ps_tra", [128, 1024], BF16)
    env["ps_trb"] = psa("ps_trb", [128, 1024], BF16)
    env["ps_fra"] = psa("ps_fra", [128, 512], F32)
    env["ps_frb"] = psa("ps_frb", [128, 512], F32)
    env["ps_a"] = psa("ps_a", [128, 512], F32)
    env["ps_b"] = psa("ps_b", [128, 512], F32)
    env["ps_c"] = psa("ps_c", [128, 512], F32)
    env["ps_d"] = psa("ps_d", [128, 512], F32)
    sems = {}
    for s in ("pe", "ve", "ac", "po", "vf", "af"):
        sems[s] = Cnt(ctx.enter_context(nc.semaphore(s)), s)
    env.update(sems)
    env["nc"] = nc
    env["ctx"] = ctx
    env["_streams"] = []
    env["bar"] = Cnt(ctx.enter_context(nc.semaphore("bar")), "bar")
    env["s_const"] = Stream(env, "cst", 1)
    return env, sems, sb, psa


def emit_load_consts(rec, env, ident_bf, identf, ones_f):
    s = env["s_const"]
    dma(rec, "sync", env["ident_sb"][:], ident_bf[:], s)
    dma(rec, "sync", env["identf_sb"][:], identf[:], s)
    tok = dma(rec, "sync", env["ones_sb"][:], ones_f[:], s)
    rec.op("vector", lambda e: e.memset(env["eps_col"][:], EPS), inc=(env["ve"], 1))
    return tok


class PhaseAlloc:

    def __init__(self, arena_ap, limit):
        self.ap = arena_ap
        self.limit = limit
        self.off = 0
        self.hwm = 0

    def reset(self):
        self.hwm = max(self.hwm, self.off)
        self.off = 0

    def sb(self, name, shape, dt=F32):
        esz = {F32: 4, BF16: 2}.get(dt, 4)
        nb = int(np.prod(shape[1:])) * esz
        nba = (nb + 15) // 16 * 16
        assert self.off + nba <= self.limit, (name, self.off, nb, self.limit)
        ap = self.ap[0:shape[0], self.off:self.off + nb].bitcast(dt)
        self.off += nba
        if len(shape) == 3:
            ap = ap.rearrange("p (a c) -> p a c", a=shape[1])
        return ap


def emit_barrier(rec, env):
    bar = env["bar"]
    gp_tok = [(c, c.v) for s in env["_streams"] if s.eng == "gpsimd" for c in s.cnts]
    sy_tok = [(c, c.v) for s in env["_streams"] if s.eng == "sync" for c in s.cnts]
    eng_tok = [(env[k], env[k].v) for k in ("pe", "ve", "ac", "po", "vf", "af")]
    rec.op("gpsimd", lambda e: e.nop(), wait=gp_tok, inc=(bar, 1))
    rec.op("sync", lambda e: e.nop(),
           wait=sy_tok + eng_tok + [(bar, bar.v)], inc=(bar, 1))
    bv = bar.v
    for eng in ENGINES:
        rec.wait(eng, bar, bv)


def mk_consts(nc):
    ident_bf = nc.inline_tensor(np.eye(128, dtype=BF), name="c_ident_bf").ap()
    identf = nc.inline_tensor(np.eye(128, dtype=np.float32), name="c_identf").ap()
    ones_f = nc.inline_tensor(np.ones((1, 128), np.float32), name="c_ones_f").ap()
    return ident_bf, identf, ones_f



def emit_ffn0(rec, env, aps, sb, psa):
    nc_x, w, bias, xo = aps["x"], aps["w"], aps["bias"], aps["xo"]
    pe, ve, ac = (env[k] for k in ("pe", "ve", "ac"))
    nc_, ctx_ = env["nc"], env["ctx"]
    s_x = Stream(env, "f0x", 2)
    s_w = Stream(env, "f0w", 2)
    s_o = Stream(env, "f0o", 2)

    bias_row = sb("f0_bias_row", [1, D])
    bias_tile = sb("f0_bias_tile", [128, D])
    wn = [sb(f"f0_wn{i}", [128, D]) for i in range(2)]
    alpha = sb("f0_alpha", [128, KT])
    sgn = sb("f0_sgn", [128, D])
    bw_row = sb("f0_bw_row", [128, D], BF16)
    bwT = sb("f0_bwT", [128, KT * D], BF16)
    x_sb = [sb(f"f0_x{i}", [128, D]) for i in range(2)]
    out_sb = [sb(f"f0_out{i}", [128, D]) for i in range(2)]
    xln = sb("f0_xln", [128, D], BF16)
    xlnT = sb("f0_xlnT", [128, D], BF16)
    ps_mm = [env["ps_a"], env["ps_b"]]

    bwT_kc = bwT[:].rearrange("p (kc o) -> p kc o", kc=KT)

    bias_tok = dma(rec, "sync", bias_row[:], bias.rearrange("(one d) -> one d", one=1),
                   env["s_const"])

    for ns in range(4):
        rec.op("tensor", lambda e, ns=ns: e.matmul(
            env["ps_c"][:], env["ones_sb"][:],
            bias_row[0:1, ns * 512:(ns + 1) * 512], start=True, stop=True),
            wait=([bias_tok] if ns == 0 else [(ve, ve.v)]),
            inc=(pe, 1))
        rec.op("vector", lambda e, ns=ns: e.tensor_copy(
            bias_tile[:, ns * 512:(ns + 1) * 512], env["ps_c"][:]),
            wait=[(pe, pe.v)], inc=(ve, 1))

    w_released = []
    bw_pe_done = 0
    for i in range(KT):
        wt = [(ve, w_released[i - 2])] if i >= 2 else []
        w_tok = dma(rec, "gpsimd", wn[i % 2][:], w[i * 128:(i + 1) * 128, :], s_w, wait=wt)
        rec.op("scalar", lambda e, i=i: e.activation(
            env["trash"][:], wn[i % 2][:], AF.Abs, accum_out=alpha[:, i:i + 1]),
            wait=[w_tok], inc=(ac, 1))
        rec.op("vector", lambda e, i=i: e.tensor_scalar(
            sgn[:], wn[i % 2][:], 0.0, 2.0, ALU.is_ge, ALU.mult),
            wait=[(ac, ac.v)] + ([(pe, bw_pe_done)] if bw_pe_done else []), flush=env["vf"])
        rec.op("vector", lambda e: e.tensor_scalar(sgn[:], sgn[:], 1.0, None, ALU.subtract), flush=env["vf"])
        rec.op("vector", lambda e, i=i: e.tensor_scalar(
            alpha[:, i:i + 1], alpha[:, i:i + 1], 1.0 / D, None, ALU.mult), flush=env["vf"])
        rec.op("vector", lambda e, i=i: e.tensor_scalar(
            bw_row[:], sgn[:], alpha[:, i:i + 1], None, ALU.mult), inc=(ve, 1))
        w_released.append(ve.v)
        bw_done = ve.v
        _, bw_pe_done = emit_transpose_pack(
            rec, env,
            src_ap_fn=lambda j: bw_row[:, j * 128:(j + 1) * 128],
            nblk=KT,
            dst_ap_fn=lambda b, nb, i=i: bwT_kc[:, b * 8:b * 8 + nb, i * 128:(i + 1) * 128],
            bf=True, wait_first=[(ve, bw_done)])
    wprep_ve = ve.v

    x_released = {}
    out_stored = {}
    evac_done = {0: 0, 1: 0}
    xln_pe_done = 0
    xlnT_consumed_pe = 0
    for t in range(NT):
        wt = [(ve, x_released[t - 2])] if t >= 2 else []
        x_tok = dma(rec, "sync", x_sb[t % 2][:], nc_x[t * 128:(t + 1) * 128, :], s_x, wait=wt)
        wl = [x_tok]
        ln_done = emit_ln(rec, env, x_sb[t % 2][:], xln[:], wl,
                          norm_extra_wait=([(pe, xln_pe_done)] if xln_pe_done else ()))
        tr_done, xln_pe_done = emit_transpose_pack(
            rec, env,
            src_ap_fn=lambda j: xln[:, j * 128:(j + 1) * 128],
            nblk=KT,
            dst_ap_fn=lambda b, nb: xlnT[:, b * 1024:b * 1024 + nb * 128].rearrange(
                "p (j c) -> p j c", j=nb),
            bf=True,
            wait_first=[(ve, ln_done), (ve, wprep_ve), (pe, xlnT_consumed_pe)])
        for ns in range(4):
            pw = [(ve, tr_done)]
            if evac_done[ns % 2]:
                pw.append((ve, evac_done[ns % 2]))
            for kc in range(KT):
                rec.op("tensor", lambda e, ns=ns, kc=kc: e.matmul(
                    ps_mm[ns % 2][:],
                    xlnT[:, kc * 128:(kc + 1) * 128],
                    bwT[:, kc * D + ns * 512: kc * D + (ns + 1) * 512],
                    start=(kc == 0), stop=(kc == KT - 1)),
                    wait=(pw if kc == 0 else ()),
                    inc=((pe, 1) if kc == KT - 1 else None))
            mm_done = pe.v
            if ns == 3:
                xlnT_consumed_pe = pe.v
            ow = [(pe, mm_done)]
            if t >= 2 and ns == 0:
                ow.append(out_stored[t - 2])
            rec.op("vector", lambda e, t=t, ns=ns: e.tensor_add(
                out_sb[t % 2][:, ns * 512:(ns + 1) * 512], ps_mm[ns % 2][:],
                bias_tile[:, ns * 512:(ns + 1) * 512]), wait=ow)
            rec.op("vector", lambda e, t=t, ns=ns: e.tensor_add(
                out_sb[t % 2][:, ns * 512:(ns + 1) * 512],
                out_sb[t % 2][:, ns * 512:(ns + 1) * 512],
                x_sb[t % 2][:, ns * 512:(ns + 1) * 512]), inc=(ve, 1))
            evac_done[ns % 2] = ve.v
        x_released[t] = ve.v
        out_stored[t] = dma(rec, "sync", xo[t * 128:(t + 1) * 128, :], out_sb[t % 2][:],
                            s_o, wait=[(ve, ve.v)])
    for cnt in s_o.cnts:
        rec.wait("sync", cnt, cnt.v)


def build_ffn0():
    nc = bass.Bass("TRN2", target_bir_lowering=False, debug=False)
    aps = {
        "x": nc.dram_tensor("x", [SEQ, D], F32, kind="ExternalInput").ap(),
        "w": nc.dram_tensor("w", [D, D], F32, kind="ExternalInput").ap(),
        "bias": nc.dram_tensor("bias", [D], F32, kind="ExternalInput").ap(),
        "xo": nc.dram_tensor("xo", [SEQ, D], F32, kind="ExternalOutput").ap(),
    }
    ident_bf, identf, ones_f = mk_consts(nc)
    ctx = ExitStack()
    with ctx:
        env, sems, sb, psa = common_env(nc, ctx)
        rec = Rec()
        emit_load_consts(rec, env, ident_bf, identf, ones_f)
        emit_ffn0(rec, env, aps, sb, psa)
        rec.check()
        block = ctx.enter_context(nc.Block())
        rec.replay(block)
    return nc



MI = 1408
MKT = MI // 128
NE = 8


def emit_moe(rec, env, aps, sb, psa, nt=NT):
    x, gwt = aps["x"], aps["gwt"]
    w1t, w3t, w2t = aps["w1t"], aps["w3t"], aps["w2t"]
    sw1t, sw3t, sw2t = aps["sw1t"], aps["sw3t"], aps["sw2t"]
    xo = aps["xo"]
    s_xlnT = aps["s_xlnT"]
    pe, ve, ac, vf, af = (env[k] for k in ("pe", "ve", "ac", "vf", "af"))
    nc_, ctx_ = env["nc"], env["ctx"]
    s_x = Stream(env, "m_x", 1)
    s_xi = Stream(env, "m_xi", 1)
    s_xs = Stream(env, "m_xs", 1)
    s_wh = Stream(env, "m_wh", 1)
    s_w2 = Stream(env, "m_w2", 1)
    s_xt = Stream(env, "m_xt", 2)
    s_ya = Stream(env, "m_ya", 2)

    gwT_sb = sb("m_gwT", [128, KT, 8])
    x_sb = [sb("m_x0", [128, D])]
    x_sb = [x_sb[0], x_sb[0]]
    xlnf = sb("m_xlnf", [128, D])
    xln_bf = sb("m_xln", [128, D], BF16)
    xlnT_bf = sb("m_xlnT", [128, D], BF16)
    xlnfT = sb("m_xlnfT", [128, D])
    sm = sb("m_sm", [128, 8])
    exl = sb("m_exl", [128, 8])
    dw_all = sb("m_dw", [128, nt * 8])
    rmax = sb("m_rmax", [128, 1])
    negmax = sb("m_negmax", [128, 1])
    ssum = sb("m_ssum", [128, 1])
    rs = sb("m_rs", [128, 1])
    eqm = sb("m_eqm", [128, 8])
    masked = sb("m_masked", [128, 8])
    keep = sb("m_keep", [128, 8])
    w1sb = sb("m_w1", [128, KT, MI], BF16)
    w3sb = sb("m_w3", [128, KT, MI], BF16)
    w2sb = sb("m_w2", [128, MKT, D], BF16)
    xt_sb = [sb(f"m_xt{i}", [128, D], BF16) for i in range(2)]
    sil = sb("m_sil", [128, 512])
    h2s = sb("m_h2s", [128, MI], BF16)
    h2sT = sb("m_h2sT", [128, MI], BF16)
    ycont = [sb(f"m_yc{i}", [128, D]) for i in range(2)]

    ps_logit = env["ps_c"]
    ps_h1 = [env["ps_a"], env["ps_c"]]
    ps_h3 = [env["ps_b"], env["ps_d"]]
    ps_y = [env["ps_fra"], env["ps_frb"]]

    gw_tok = dma(rec, "sync", gwT_sb[:], gwt.rearrange("(kc p) e -> p kc e", p=128),
                 env["s_const"])

    xlnT_stored = {}
    xo_init_so = {}
    x_rel = {}
    logit_pe = 0
    xlnT_pe = 0
    for t in range(nt):
        wt = [(ve, x_rel[t - 1]), xo_init_so[t - 1]] if t >= 1 else []
        x_tok = dma(rec, "sync", x_sb[t % 2][:], x[t * 128:(t + 1) * 128, :], s_x, wait=wt)
        wl = [x_tok]
        emit_ln(rec, env, x_sb[t % 2][:], xlnf[:], wl,
                norm_extra_wait=([(env["pe"], env["m_xlnf_pe"])] if env.get("m_xlnf_pe") else ()))
        xo_init_so[t] = dma(rec, "sync", xo[t * 128:(t + 1) * 128, :], x_sb[t % 2][:],
                            s_xi, wait=[(ve, ve.v)])
        x_rel[t] = ve.v
        rec.op("vector", lambda e: e.tensor_copy(xln_bf[:], xlnf[:]),
               wait=([(pe, env["m_xlnbf_pe"])] if env.get("m_xlnbf_pe") else ()),
               flush=vf, inc=(ve, 1))
        cast_done = ve.v
        trv, trpe = emit_transpose_pack(
            rec, env, src_ap_fn=lambda j: xln_bf[:, j * 128:(j + 1) * 128],
            nblk=KT,
            dst_ap_fn=lambda b, nb: xlnT_bf[:, b * 1024:b * 1024 + nb * 128].rearrange(
                "p (j c) -> p j c", j=nb),
            bf=True, wait_first=[(ve, cast_done)] +
                ([xlnT_stored[t - 1]] if t >= 1 else []))
        env["m_xlnbf_pe"] = trpe
        xlnT_stored[t] = dma(rec, "sync", s_xlnT[t, :, :], xlnT_bf[:], s_xs,
                             wait=[(ve, trv)])
        trvf, trpef = emit_transpose_pack(
            rec, env, src_ap_fn=lambda j: xlnf[:, j * 128:(j + 1) * 128],
            nblk=KT,
            dst_ap_fn=lambda b, nb: xlnfT[:, b * 512:b * 512 + nb * 128].rearrange(
                "p (j c) -> p j c", j=nb),
            bf=False, wait_first=[(ve, ve.v), (pe, logit_pe)])
        env["m_xlnf_pe"] = trpef
        for kc in range(KT):
            rec.op("tensor", lambda e, kc=kc: e.matmul(
                ps_logit[:, 0:8], xlnfT[:, kc * 128:(kc + 1) * 128],
                gwT_sb[:, kc, :], start=(kc == 0), stop=(kc == KT - 1)),
                wait=([(ve, trvf), gw_tok, (ve, env.get("m_dw_ve", 0)), (ac, env.get("m_exp_ac", 0))] if kc == 0 else ()),
                inc=((pe, 1) if kc == KT - 1 else None))
        logit_pe = pe.v
        rec.op("vector", lambda e: e.reduce_max(rmax[:], ps_logit[:, 0:8], axis=AX.X),
               wait=[(pe, logit_pe)], flush=vf)
        rec.op("vector", lambda e: e.tensor_scalar(negmax[:], rmax[:], -1.0, None, ALU.mult),
               flush=vf, inc=(ve, 1))
        rec.op("scalar", lambda e: e.activation(exl[:], ps_logit[:, 0:8], AF.Exp,
                                                bias=negmax[:]),
               wait=[(ve, ve.v)], flush=af, inc=(ac, 1))
        env["m_exp_ac"] = ac.v
        rec.op("vector", lambda e: e.reduce_sum(ssum[:], exl[:], axis=AX.X),
               wait=[(ac, ac.v)], flush=vf)
        rec.op("vector", lambda e: e.reciprocal(rs[:], ssum[:]), flush=vf)
        rec.op("vector", lambda e: e.tensor_scalar(sm[:], exl[:], rs[:], None, ALU.mult),
               flush=vf)
        rec.op("vector", lambda e: e.reduce_max(rmax[:], sm[:], axis=AX.X), flush=vf)
        rec.op("vector", lambda e: e.tensor_scalar(eqm[:], sm[:], rmax[:], None, ALU.is_equal),
               flush=vf)
        rec.op("vector", lambda e: e.scalar_tensor_tensor(
            masked[:], eqm[:], -1e30, sm[:], ALU.mult, ALU.add), flush=vf)
        rec.op("vector", lambda e: e.reduce_max(rmax[:], masked[:], axis=AX.X), flush=vf)
        rec.op("vector", lambda e: e.tensor_scalar(keep[:], sm[:], rmax[:], None, ALU.is_ge),
               flush=vf)
        rec.op("vector", lambda e, t=t: e.tensor_mul(dw_all[:, t * 8:(t + 1) * 8], sm[:], keep[:]),
               inc=(ve, 1), flush=vf)
        env["m_dw_ve"] = ve.v
    m0_ve = ve.v
    xlnT_all_tok = xlnT_stored[nt - 1]
    init_all_tok = xo_init_so[nt - 1]
    if "dbg_dw" in aps:
        dma(rec, "sync", aps["dbg_dw"][:], dw_all[:, 0:nt * 8], s_xi, wait=[(ve, m0_ve)])

    passes = []
    for e in range(NE):
        passes.append((w1t[e], w3t[e], w2t[e], e))
    passes.append((sw1t[:, 0:MI], sw3t[:, 0:MI], sw2t[0:MI, :], None))
    passes.append((sw1t[:, MI:2 * MI], sw3t[:, MI:2 * MI], sw2t[MI:2 * MI, :], None))
    which = aps.get("which")
    if which is not None:
        passes = [passes[i] for i in which]

    slabs3 = [(0, 512), (512, 512), (1024, 384)]
    last_pass_pe = 0
    xt_rel = {}
    yc_rel = {}
    for pi, (pw1, pw3, pw2, dwi) in enumerate(passes):
        wwt = [(pe, env.get("m_hdone", 0))] if pi else [(ve, m0_ve)]
        dma(rec, "gpsimd", w1sb[:], pw1.rearrange("(kc p) i -> p kc i", p=128), s_wh, wait=wwt)
        wh_tok = dma(rec, "gpsimd", w3sb[:], pw3.rearrange("(kc p) i -> p kc i", p=128), s_wh)
        w2_tok = dma(rec, "gpsimd", w2sb[:], pw2.rearrange("(kc p) o -> p kc o", p=128), s_w2,
                     wait=([(pe, last_pass_pe)] if pi else []))
        for t in range(nt):
            g = pi * nt + t
            wt = [xlnT_all_tok]
            if g >= 2:
                wt.append((pe, xt_rel[g - 2]))
            xt_tok = dma(rec, "sync", xt_sb[t % 2][:], s_xlnT[t, :, :], s_xt, wait=wt)
            h2s_parts_ve = []
            for si, (s0, sw) in enumerate(slabs3):
                q = si % 2
                g2 = pi * nt * 3 + t * 3 + si
                mmw = [xt_tok, wh_tok]
                if env.get(("m_ps_h_ve", q), 0):
                    mmw.append((ve, env[("m_ps_h_ve", q)]))
                if env.get(("m_ps_h_ac", q), 0):
                    mmw.append((ac, env[("m_ps_h_ac", q)]))
                for kc in range(KT):
                    rec.op("tensor", lambda e, kc=kc, s0=s0, sw=sw, t=t, q=q: e.matmul(
                        ps_h1[q][:, 0:sw], xt_sb[t % 2][:, kc * 128:(kc + 1) * 128],
                        w1sb[:, kc, s0:s0 + sw],
                        start=(kc == 0), stop=(kc == KT - 1)),
                        wait=(mmw if kc == 0 else ()))
                for kc in range(KT):
                    rec.op("tensor", lambda e, kc=kc, s0=s0, sw=sw, t=t, q=q: e.matmul(
                        ps_h3[q][:, 0:sw], xt_sb[t % 2][:, kc * 128:(kc + 1) * 128],
                        w3sb[:, kc, s0:s0 + sw],
                        start=(kc == 0), stop=(kc == KT - 1)),
                        inc=((pe, 1) if kc == KT - 1 else None))
                hmm_done = pe.v
                rec.op("scalar", lambda e, sw=sw, q=q: e.activation(
                    sil[:, 0:sw], ps_h1[q][:, 0:sw], AF.Silu),
                       wait=[(pe, hmm_done), (ve, env.get(("m_ps_h_ve", q), 0))], inc=(ac, 1))
                env[("m_ps_h_ac", q)] = ac.v
                if dwi is not None:
                    rec.op("vector", lambda e, s0=s0, sw=sw, t=t, dwi=dwi, q=q: e.scalar_tensor_tensor(
                        h2s[:, s0:s0 + sw], sil[:, 0:sw], dw_all[:, t * 8 + dwi:t * 8 + dwi + 1],
                        ps_h3[q][:, 0:sw], ALU.mult, ALU.mult),
                        wait=[(ac, ac.v)], inc=(ve, 1))
                else:
                    rec.op("vector", lambda e, s0=s0, sw=sw, q=q: e.tensor_mul(
                        h2s[:, s0:s0 + sw], sil[:, 0:sw], ps_h3[q][:, 0:sw]),
                        wait=[(ac, ac.v)], inc=(ve, 1))
                env[("m_ps_h_ve", q)] = ve.v
                h2s_parts_ve.append(ve.v)
            env["m_hdone"] = pe.v

            trv, trpe = emit_transpose_pack(
                rec, env, src_ap_fn=lambda j: h2s[:, j * 128:(j + 1) * 128],
                nblk=MKT,
                dst_ap_fn=lambda b, nb: h2sT[:, b * 1024:b * 1024 + nb * 128].rearrange(
                    "p (j c) -> p j c", j=nb),
                bf=True, wait_first=[(ve, h2s_parts_ve[-1])])
            for ns in range(4):
                q = ns % 2
                yw = [(ve, trv), w2_tok]
                if env.get(("m_ps_y_ve", q), 0):
                    yw.append((ve, env[("m_ps_y_ve", q)]))
                if pi == 0 and t == 0 and ns < 2:
                    yw.append((ve, m0_ve))
                for kc in range(MKT):
                    rec.op("tensor", lambda e, kc=kc, ns=ns, q=q: e.matmul(
                        ps_y[q][:], h2sT[:, kc * 128:(kc + 1) * 128],
                        w2sb[:, kc, ns * 512:(ns + 1) * 512],
                        start=(kc == 0), stop=(kc == MKT - 1)),
                        wait=(yw if kc == 0 else ()),
                        inc=((pe, 1) if kc == MKT - 1 else None))
                ymm = pe.v
                ycw = [(pe, ymm)]
                if g >= 2 and ns == 0:
                    ycw.append(yc_rel[g - 2])
                rec.op("vector", lambda e, ns=ns, t=t, q=q: e.tensor_copy(
                    ycont[t % 2][:, ns * 512:(ns + 1) * 512], ps_y[q][:]),
                    wait=ycw, inc=(ve, 1))
                env[("m_ps_y_ve", q)] = ve.v
            env["m_ps_y_ve"] = ve.v
            xt_rel[g] = pe.v
            yc_rel[g] = dma(rec, "gpsimd", xo[t * 128:(t + 1) * 128, :],
                            ycont[t % 2][:], s_ya,
                            wait=[(ve, env["m_ps_y_ve"])] +
                                 ([init_all_tok] if pi == 0 else []),
                            accum_op=ALU.add)
        last_pass_pe = pe.v

    for cnt in s_ya.cnts:
        rec.wait("gpsimd", cnt, cnt.v)


def build_moe(nt=NT, dbg=False, which=None):
    nc = bass.Bass("TRN2", target_bir_lowering=False, debug=False)
    aps = {
        "x": nc.dram_tensor("x", [SEQ, D], F32, kind="ExternalInput").ap(),
        "gwt": nc.dram_tensor("gwt", [D, 8], F32, kind="ExternalInput").ap(),
        "w1t": nc.dram_tensor("w1t", [NE, D, MI], F32, kind="ExternalInput").ap(),
        "w3t": nc.dram_tensor("w3t", [NE, D, MI], F32, kind="ExternalInput").ap(),
        "w2t": nc.dram_tensor("w2t", [NE, MI, D], F32, kind="ExternalInput").ap(),
        "sw1t": nc.dram_tensor("sw1t", [D, 2 * MI], F32, kind="ExternalInput").ap(),
        "sw3t": nc.dram_tensor("sw3t", [D, 2 * MI], F32, kind="ExternalInput").ap(),
        "sw2t": nc.dram_tensor("sw2t", [2 * MI, D], F32, kind="ExternalInput").ap(),
        "xo": nc.dram_tensor("xo", [SEQ, D], F32, kind="ExternalOutput").ap(),
        "s_xlnT": nc.dram_tensor("s_xlnT", [NT, 128, D], BF16, kind="Internal").ap(),
    }
    if dbg:
        aps["dbg_dw"] = nc.dram_tensor("dbg_dw", [128, nt * 8], F32, kind="ExternalOutput").ap()
    aps["which"] = which
    ident_bf, identf, ones_f = mk_consts(nc)
    ctx = ExitStack()
    with ctx:
        env, sems, sb, psa = common_env(nc, ctx)
        rec = Rec()
        emit_load_consts(rec, env, ident_bf, identf, ones_f)
        emit_moe(rec, env, aps, sb, psa, nt=nt)
        rec.check()
        block = ctx.enter_context(nc.Block())
        rec.replay(block)
    return nc



INTER = 4096
NH = 64
HD = 64
NST = 128
CHUNK = 256
NCH = SEQ // CHUNK
CDIM = 4352
HS_T = 34
GATE_T = 32


def mamba_consts(nc, tag):
    trim0 = np.zeros((128, 256), dtype=BF)
    trim1 = np.zeros((128, 256), dtype=BF)
    for p in range(128):
        trim0[p, p:] = 1
        trim1[p, 128 + p:] = 1
    tri0f = np.triu(np.ones((128, 128), np.float32))
    onesf = np.ones((128, 128), np.float32)
    return {
        "ones_bf": nc.inline_tensor(np.ones((1, 128), dtype=BF), name=f"{tag}_onesbf").ap(),
        "trim0": nc.inline_tensor(trim0, name=f"{tag}_trim0").ap(),
        "trim1": nc.inline_tensor(trim1, name=f"{tag}_trim1").ap(),
        "tri0f": nc.inline_tensor(tri0f, name=f"{tag}_tri0f").ap(),
        "onesf": nc.inline_tensor(onesf, name=f"{tag}_onesf").ap(),
    }


def emit_mamba(rec, env, aps, sb, psa, variant, cst, nt=NT, phases="abc", scan_chunks=None, nheads=NH, do_post=True, upto=99):
    x, w_in, conv_w, conv_b = aps["x"], aps["w_in"], aps["conv_w"], aps["conv_b"]
    dt_bias, a_log, dvec, norm_w, w_out = (aps[k] for k in
                                           ("dt_bias", "a_log", "dvec", "norm_w", "w_out"))
    xo = aps["xo"]
    s_hsT, s_gateT, s_dt, s_ynT, s_acum = (aps[k] for k in
                                           ("s_hsT", "s_gateT", "s_dt", "s_ynT", "s_acum"))
    pe, ve, ac, vf, af = (env[k] for k in ("pe", "ve", "ac", "vf", "af"))
    po = env["po"]
    nc_, ctx_ = env["nc"], env["ctx"]
    _sp = "mb" + variant
    st_xa = Stream(env, _sp + "xa", 2)
    st_wf = Stream(env, _sp + "wf", 2)
    st_fs = Stream(env, _sp + "fs", 2)
    st_ds = Stream(env, _sp + "ds", 1)
    st_hs = Stream(env, _sp + "hs", 1)
    st_dt = Stream(env, _sp + "dt", 1)
    st_as = Stream(env, _sp + "as", 1)
    st_arh = Stream(env, _sp + "arh", 2)
    st_arl = Stream(env, _sp + "arl", 2)
    st_gt = Stream(env, _sp + "gt", 1)
    st_ys = Stream(env, _sp + "ys", 1)
    st_wk = Stream(env, _sp + "wk", 2)
    st_yt = Stream(env, _sp + "yt", 2)
    st_xc = Stream(env, _sp + "xc", 2)
    st_os = Stream(env, _sp + "os", 2)
    V = variant
    tg = "mb" + V

    arena_bf = sb(tg + "_abf", [128, 43520], BF16)
    arena_f = sb(tg + "_af", [128, 8704])

    def abf(off, shape):
        n = int(np.prod(shape[1:]))
        ap = arena_bf[0:shape[0], off:off + n]
        if len(shape) == 3:
            ap = ap.rearrange("p (a b) -> p a b", a=shape[1])
        return ap

    def afr(off, shape):
        n = int(np.prod(shape[1:]))
        ap = arena_f[0:shape[0], off:off + n]
        if len(shape) == 3:
            ap = ap.rearrange("p (a b) -> p a b", a=shape[1])
        return ap

    xlnT_blk = abf(0, [128, KT, 2048])
    wfc = [abf(32768 + i * 2048, [128, KT, 128]) for i in range(2)]
    fme = [abf(36864 + i * 512, [128, 512]) for i in range(2)]
    xa = [afr(i * 2048, [128, 2048]) for i in range(2)]
    dte = afr(4096, [64, 512])
    hs_sb = abf(0, [128, HS_T, 259])
    ynT = afr(0, [128, GATE_T // 2, 256])
    ynT = arena_f[0:128, 0:GATE_T * 128].bitcast(BF16).rearrange(
        "p (a c) -> p a c", a=GATE_T)
    xsbc = abf(8832, [128, HS_T, 256])
    gate_fm = abf(8832, [128, GATE_T, 256])
    B_fm = xsbc[:, 32, :]
    C_fm = xsbc[:, 33, :]
    X_tm = [abf(17536 + i * 4096, [128, 4096]) for i in range(2)]
    xdt_tm = [abf(25728 + i * 4096, [128, 4096]) for i in range(2)]
    Xdec_tm = [abf(33920 + i * 4096, [128, 4096]) for i in range(2)]
    gate_tm = Xdec_tm
    conv_acc = afr(0, [128, HS_T, 256])
    wns = abf(0, [128, 32, 512])
    ynTt = [abf(16384 + i * 4096, [128, 32, 128]) for i in range(2)]
    xcs = [afr(i * 512, [128, 512]) for i in range(2)]
    outs = [afr(1024 + i * 512, [128, 512]) for i in range(2)]
    wkf = [afr(2048 + i * 512, [128, 512]) for i in range(2)]

    acol = sb(tg + "_acol", [64, 1])
    dtbcol = sb(tg + "_dtb", [64, 1])
    dtc = sb(tg + "_dtc", [64, 256])
    spf = sb(tg + "_spf", [64, 256])
    decf = sb(tg + "_decf", [64, 256])
    ddecf = sb(tg + "_ddecf", [64, 256])
    acumf = sb(tg + "_acumf", [64, 256])
    adt_tm = sb(tg + "_adttm", [128, 128])
    acum_tm = sb(tg + "_acumtm", [128, 128])
    nacum_tm = sb(tg + "_nacumtm", [128, 128])
    sp_tm = sb(tg + "_sptm", [128, 128])
    ddec_tm = sb(tg + "_ddectm", [128, 128])
    ehb = [sb(tg + f"_ehb{p}", [128, 512], BF16) for p in range(2)]
    Mh = [[sb(tg + f"_mh{p}{i}", [128, 256], BF16) for i in range(2)] for p in range(2)]
    exn = [sb(tg + f"_exn{p}", [128, 256], BF16) for p in range(2)]
    chp = [sb(tg + f"_chp{p}", [128, 256], BF16) for p in range(2)]
    scoresm = [sb(tg + f"_scm{i}", [128, 256], BF16) for i in range(2)]
    e_tile = sb(tg + "_etile", [128, 64])
    arow = sb(tg + "_arow", [1, 64])
    erow = sb(tg + "_erow", [1, 64])
    acrowh = [sb(tg + f"_acrh{i}", [1, 2048], BF16) for i in range(2)]
    acrowl = [sb(tg + f"_acrl{i}", [1, 2048], BF16) for i in range(2)]
    acumhi = sb(tg + "_acumhi", [64, 256], BF16)
    acumlo = sb(tg + "_acumlo", [64, 256], BF16)
    ones_bf = sb(tg + "_onesbf", [1, 128], BF16)
    y_bf = [sb(tg + f"_ybf{i}", [128, 4096], BF16) for i in range(2)]
    D_tile = sb(tg + "_dtile", [128, 64])
    D_row = sb(tg + "_drow", [1, 64])
    convw = sb(tg + "_convw", [128, HS_T, 4])
    convb = sb(tg + "_convb", [128, HS_T])
    S = sb(tg + "_S", [128, 4096])
    S_bf = sb(tg + "_Sbf", [128, 4096], BF16)
    sil_half = sb(tg + "_sil", [128, 2048], BF16)
    B_tm = sb(tg + "_btm", [128, 256], BF16)
    trim_sb = sb(tg + "_trim", [128, 512], BF16)
    tri0f_sb = sb(tg + "_tri0f", [128, 128])
    onesf_sb = sb(tg + "_onesf", [128, 128])
    nw_sb = sb(tg + "_nw", [128, 32])
    vc0 = sb(tg + "_vc0", [128, 1])
    vc1 = sb(tg + "_vc1", [128, 1])
    y_tm = y_bf

    ps_tra, ps_trb = env["ps_tra"], env["ps_trb"]
    ps_fra, ps_frb = env["ps_fra"], env["ps_frb"]
    ps_a, ps_b, ps_c, ps_d = env["ps_a"], env["ps_b"], env["ps_c"], env["ps_d"]

    sc = env["s_const"]
    dma(rec, "sync", trim_sb[:, 0:256], cst["trim0"][:], sc)
    dma(rec, "sync", trim_sb[:, 256:512], cst["trim1"][:], sc)
    dma(rec, "sync", tri0f_sb[:], cst["tri0f"][:], sc)
    dma(rec, "sync", onesf_sb[:], cst["onesf"][:], sc)
    dma(rec, "sync", acol[:], a_log.rearrange("(h one) -> h one", one=1), sc)
    dma(rec, "sync", dtbcol[:], dt_bias.rearrange("(h one) -> h one", one=1), sc)
    dma(rec, "sync", D_row[:], dvec.rearrange("(one h) -> one h", one=1), sc)
    dma(rec, "sync", convw[:], conv_w.rearrange("(c p) k -> p c k", p=128), sc)
    dma(rec, "sync", convb[:], conv_b.rearrange("(c p) -> p c", p=128), sc,
        allow_slow_non_contiguous=True)
    dma(rec, "sync", nw_sb[:], norm_w.rearrange("(c p) -> p c", p=128), sc,
        allow_slow_non_contiguous=True)
    cst_tok = dma(rec, "sync", ones_bf[:], cst["ones_bf"][:], sc)
    rec.op("scalar", lambda e: e.activation(acol[:], acol[:], AF.Exp),
           wait=[cst_tok], flush=af)
    rec.op("scalar", lambda e: e.mul(acol[:], acol[:], -1.0), flush=af, inc=(ac, 1))
    acol_done = ac.v
    rec.op("tensor", lambda e: e.matmul(ps_c[:, 0:64], env["ones_sb"][:], D_row[:],
                                        start=True, stop=True),
           wait=[cst_tok], inc=(pe, 1))
    rec.op("vector", lambda e: e.tensor_copy(D_tile[:], ps_c[:, 0:64]),
           wait=[(pe, pe.v)], inc=(ve, 1), flush=env["vf"])
    rec.op("vector", lambda e: e.memset(S[:], 0.0))
    rec.op("vector", lambda e: e.memset(S_bf[:], 0.0), inc=(ve, 1))
    init_ve = ve.v
    env[(tg, "y_evac_ve", 0)] = init_ve

    NBLK = nt // 16
    phA_store = {}
    hs_pe = 0
    for tb in range(NBLK):
        for tt in range(16):
            t = tb * 16 + tt
            wt = []
            if t >= 2:
                wt.append((ve, env.get((tg, "xa_rel", t - 2), 0)))
            xa_tok = dma(rec, "sync", xa[t % 2][:], x[t * 128:(t + 1) * 128, :],
                         st_xa, wait=wt)
            wl = [xa_tok]
            if env.get((tg, "xln_pe")):
                wl.append((pe, env[(tg, "xln_pe")]))
            ln_done = emit_ln(rec, env, xa[t % 2][:], env["trash"][:, 0:D], wl,
                              norm_extra_wait=([(pe, env.get((tg, "xln_pe"), 0))]
                                               if env.get((tg, "xln_pe")) else ()))
            env[(tg, "xa_rel", t)] = ve.v
            trv, trpe = emit_transpose_pack(
                rec, env, src_ap_fn=lambda j: env["trash"][:, j * 128:(j + 1) * 128],
                nblk=KT,
                dst_ap_fn=lambda b, nb, tt=tt: xlnT_blk[:, b * 8:b * 8 + nb,
                                                        tt * 128:(tt + 1) * 128],
                bf=True,
                wait_first=[(ve, ln_done), (pe, env.get((tg, "blk_mm_pe"), 0))])
            env[(tg, "xln_pe")] = trpe
        xlnT_ve = ve.v
        for fc in range(67):
            fw = 128 if fc < 66 else 64
            wwait = [(pe, env.get((tg, "wfc_pe", fc % 2), 0))]
            w_tok = dma(rec, "gpsimd", wfc[fc % 2][:, :, 0:fw],
                        w_in[:, fc * 128:fc * 128 + fw].rearrange("(kc p) f -> p kc f", p=128),
                        st_wf, wait=wwait)
            for ts4 in range(4):
                psm = [ps_a, ps_b][ts4 % 2]
                mm_w = [w_tok, (ve, xlnT_ve),
                        (ve, env.get((tg, "fme_ve", ts4 % 2), 0))]
                for kc in range(KT):
                    rec.op("tensor", lambda e, fc=fc, fw=fw, ts4=ts4, kc=kc, psm=psm: e.matmul(
                        psm[0:fw, :],
                        wfc[fc % 2][:, kc, 0:fw],
                        xlnT_blk[:, kc, ts4 * 512:(ts4 + 1) * 512],
                        start=(kc == 0), stop=(kc == KT - 1)),
                        wait=(mm_w if kc == 0 else ()),
                        inc=((pe, 1) if kc == KT - 1 else None))
                env[(tg, "wfc_pe", fc % 2)] = pe.v
                env[(tg, "blk_mm_pe")] = pe.v
                mm_done = pe.v
                tok0 = tb * 2048 + ts4 * 512
                if fc < 66:
                    ew = [(pe, mm_done)]
                    if (tg, "fme_so", ts4 % 2) in env:
                        ew.append(env[(tg, "fme_so", ts4 % 2)])
                    rec.op("vector", lambda e, fw=fw, ts4=ts4, psm=psm: e.tensor_copy(
                        fme[ts4 % 2][0:fw, :], psm[0:fw, :]), wait=ew, inc=(ve, 1))
                    env[(tg, "fme_ve", ts4 % 2)] = ve.v
                    if fc < 32:
                        dst = s_gateT[fc, :, tok0:tok0 + 512]
                    else:
                        dst = s_hsT[fc - 32, :, tok0:tok0 + 512]
                    env[(tg, "fme_so", ts4 % 2)] = dma(
                        rec, "sync", dst, fme[ts4 % 2][:], st_fs, wait=[(ve, ve.v)])
                else:
                    ew = [(pe, mm_done)]
                    if (tg, "dte_so") in env:
                        ew.append(env[(tg, "dte_so")])
                    rec.op("vector", lambda e, ts4=ts4, psm=psm: e.tensor_copy(
                        dte[:], psm[0:64, :]), wait=ew, inc=(ve, 1))
                    env[(tg, "fme_ve", ts4 % 2)] = ve.v
                    env[(tg, "dte_so")] = dma(rec, "sync", s_dt[:, tok0:tok0 + 512],
                                              dte[:], st_ds, wait=[(ve, ve.v)])
    phA_pe = pe.v
    phA_tok = [(c, c.v) for c in st_fs.cnts] + [(st_ds.cnts[0], st_ds.cnts[0].v)]

    if phases == "a":
        for c_, v_ in phA_tok:
            rec.wait("sync", c_, v_)
        return
    nchunks = nt * 128 // CHUNK if scan_chunks is None else scan_chunks
    for c in range(nchunks):
        t0 = c * CHUNK
        ck = (tg, "ck")
        hw = (phA_tok + [(pe, phA_pe)]) if c == 0 else \
             [(ve, env[(tg, "conv_ve")])]
        fsp = [(0, 9), (9, 9), (18, 8), (26, 8)]
        if c == 0:
            rec.op("vector", lambda e: e.memset(hs_sb[:, :, 0:3], 0.0),
                   wait=[(pe, phA_pe)], inc=(ve, 1), flush=vf)
            hw.append((ve, ve.v))
            for (f0, fn_) in fsp:
                hs_tok = dma(rec, "sync", hs_sb[:, f0:f0 + fn_, 3:259],
                             s_hsT[f0:f0 + fn_, :, 0:256].rearrange("c p t -> p c t"),
                             st_hs, wait=hw)
                hw = []
        else:
            for (f0, fn_) in fsp:
                hs_tok = dma(rec, "sync", hs_sb[:, f0:f0 + fn_, 0:259],
                             s_hsT[f0:f0 + fn_, :, t0 - 3:t0 + 256].rearrange("c p t -> p c t"),
                             st_hs, wait=hw)
                hw = []
        dt_tok = dma(rec, "sync", dtc[:], s_dt[:, t0:t0 + 256], st_dt,
                     wait=[(ac, env.get((tg, "dt_ac"), 0)),
                           (ve, env.get((tg, "adt_prev"), 0))] +
                          ([phA_tok[-1]] if c == 0 else []))

        cw = [hs_tok, (ac, env.get((tg, "silu_ac"), 0))]
        if (tg, "ynT_so") in env:
            cw.append(env[(tg, "ynT_so")])
        for k in range(4):
            for j in range(HS_T):
                if k == 0:
                    rec.op("vector", lambda e, j=j: e.tensor_scalar(
                        conv_acc[:, j, :], hs_sb[:, j, 0:256],
                        convw[:, j, 0:1], None, ALU.mult),
                        wait=(cw if j == 0 else ()),
                        flush=(vf if j == HS_T - 1 else None))
                else:
                    rec.op("vector", lambda e, j=j, k=k: e.scalar_tensor_tensor(
                        conv_acc[:, j, :], hs_sb[:, j, k:k + 256],
                        convw[:, j, k:k + 1],
                        conv_acc[:, j, :], ALU.mult, ALU.add),
                        inc=((ve, 1) if (j == HS_T - 1 and k == 3) else None),
                        flush=(vf if (j == HS_T - 1 and k != 3) else None))
        env[(tg, "conv_ve")] = ve.v
        conv_ve = ve.v
        sw = [(ve, conv_ve), (pe, env.get((tg, "xtr_pe"), 0)),
              (po, env.get((tg, "scm_cons_po"), 0))]
        for j in range(HS_T):
            rec.op("scalar", lambda e, j=j: e.activation(
                xsbc[:, j, :], conv_acc[:, j, :], AF.Silu,
                bias=convb[:, j:j + 1]),
                wait=(sw if j == 0 else ()),
                inc=((ac, 1) if j == HS_T - 1 else None))
        env[(tg, "silu_ac")] = ac.v
        silu_ac = ac.v

        if upto < 2:
            continue
        xtr_w = [(ac, silu_ac), (ve, env.get((tg, "xdt_ve"), 0))]
        for st in range(2):
            trv, trpe = emit_transpose_pack(
                rec, env,
                src_ap_fn=lambda j, st=st: xsbc[:, j, st * 128:(st + 1) * 128],
                nblk=32,
                dst_ap_fn=lambda b, nb, st=st: X_tm[st][:, b * 1024:b * 1024 + nb * 128]
                    .rearrange("p (j c) -> p j c", j=nb),
                bf=True, wait_first=xtr_w)
        for st in range(2):
            rec.op("tensor", lambda e, st=st: e.transpose(
                ps_tra[:, st * 128:(st + 1) * 128].rearrange("p (a) -> p a") if False else
                ps_tra[:, st * 128:(st + 1) * 128],
                B_fm[st * 128:(st + 1) * 128] if False else xsbc[:, 32, st * 128:(st + 1) * 128],
                env["ident_sb"][:]),
                wait=([(ve, env["tr_bank_ve"].get(("bf", 0), 0))] if st == 0 else ()),
                inc=((pe, 1) if st == 1 else None))
        rec.op("vector", lambda e: e.tensor_copy(B_tm[:], ps_tra[:, 0:256]),
               wait=[(pe, pe.v)], inc=(ve, 1), flush=vf)
        env["tr_bank_ve"][("bf", 0)] = ve.v
        env[(tg, "xtr_pe")] = pe.v
        btm_ve = ve.v

        if upto < 3:
            continue
        dw_ = [dt_tok, (ac, acol_done), (ve, env.get((tg, "spf_cons_ve"), 0)),
               (pe, env.get((tg, "dt_tr_pe"), 0))]
        rec.op("scalar", lambda e: e.activation(spf[:], dtc[:], AF.Abs, bias=dtbcol[:]),
               wait=dw_, flush=af)
        rec.op("scalar", lambda e: e.activation(spf[:], spf[:], AF.Exp, scale=-1.0),
               flush=af)
        rec.op("scalar", lambda e: e.activation(spf[:], spf[:], AF.Ln, bias=1.0),
               flush=af, inc=(ac, 1))
        env[(tg, "dt_ac")] = ac.v
        sp_ln_ac = ac.v
        rec.op("vector", lambda e: e.tensor_scalar(decf[:], dtc[:], dtbcol[:], 0.0,
                                                   ALU.add, ALU.max), flush=vf)
        rec.op("vector", lambda e: e.tensor_add(spf[:], spf[:], decf[:]),
               wait=[(ac, sp_ln_ac)], flush=vf)
        if V == "l1":
            rec.op("vector", lambda e: e.tensor_scalar(spf[:], spf[:], 100.0, None,
                                                       ALU.min), flush=vf)
        rec.op("vector", lambda e: e.tensor_scalar(ddecf[:], spf[:], acol[:], None,
                                                   ALU.mult), flush=vf, inc=(ve, 1))
        adt_ve = ve.v
        env[(tg, "adt_prev")] = adt_ve
        for st in range(2):
            rec.op("tensor", lambda e, st=st: e.transpose(
                ps_fra[:, st * 64:(st + 1) * 64],
                ddecf[:, st * 128:(st + 1) * 128],
                env["identf_sb"][0:64, 0:64]),
                wait=([(ve, adt_ve), (ve, env["tr_bank_ve"].get(("fr", 0), 0))]
                      if st == 0 else ()),
                inc=((pe, 1) if st == 1 else None))
        rec.op("vector", lambda e: e.tensor_copy(adt_tm[:], ps_fra[:, 0:128]),
               wait=[(pe, pe.v)], inc=(ve, 1), flush=vf)
        env["tr_bank_ve"][("fr", 0)] = ve.v
        adttm_ve = ve.v
        aw = [(ve, adttm_ve),
              (ac, env.get((tg, "ehb_ac", 0), 0)),
              (ac, env.get((tg, "ehb_ac", 1), 0)),
              (ac, env.get((tg, "exn_ac", 0), 0)),
              (ac, env.get((tg, "exn_ac", 1), 0))]
        rec.op("tensor", lambda e: e.matmul(ps_b[:, 0:64], tri0f_sb[:],
                                            adt_tm[:, 0:64], start=True, stop=True),
               wait=aw)
        rec.op("tensor", lambda e: e.matmul(ps_b[:, 64:128], onesf_sb[:],
                                            adt_tm[:, 0:64], start=True, stop=False))
        rec.op("tensor", lambda e: e.matmul(ps_b[:, 64:128], tri0f_sb[:],
                                            adt_tm[:, 64:128], start=False, stop=True),
               inc=(pe, 1))
        acum_pe = pe.v
        rec.op("vector", lambda e: e.tensor_copy(acum_tm[:], ps_b[:, 0:128]),
               wait=[(pe, acum_pe)], flush=vf)
        rec.op("vector", lambda e: e.tensor_scalar(nacum_tm[:], acum_tm[:], -1.0, None,
                                                   ALU.mult), inc=(ve, 1))
        acumtm_ve = ve.v
        for lt in range(2):
            rec.op("tensor", lambda e, lt=lt: e.transpose(
                ps_frb[0:64, lt * 128:(lt + 1) * 128],
                acum_tm[:, lt * 64:(lt + 1) * 64],
                env["identf_sb"][:]),
                wait=([(ve, acumtm_ve), (ve, env["tr_bank_ve"].get(("fr", 1), 0))]
                      if lt == 0 else ()),
                inc=((pe, 1) if lt == 1 else None))
        rec.op("vector", lambda e: e.tensor_copy(acumf[:], ps_frb[0:64, 0:256]),
               wait=[(pe, pe.v)], flush=vf, inc=(ve, 1))
        env["tr_bank_ve"][("fr", 1)] = ve.v
        acumf_ve = ve.v
        rec.op("scalar", lambda e: e.activation(decf[:], acumf[:], AF.Exp, scale=-1.0,
                                                bias=acumf[:, 255:256]),
               wait=[(ve, acumf_ve)], flush=af, inc=(ac, 1))
        dec_ac = ac.v
        rec.op("vector", lambda e: e.tensor_mul(ddecf[:], spf[:], decf[:]),
               wait=[(ac, dec_ac)], inc=(ve, 1))
        env[(tg, "spf_cons_ve")] = ve.v
        ddec_ve = ve.v
        for (src, dst) in ((spf, sp_tm), (ddecf, ddec_tm)):
            for st in range(2):
                rec.op("tensor", lambda e, src=src, st=st, dst=dst: e.transpose(
                    ps_fra[:, st * 64:(st + 1) * 64],
                    src[:, st * 128:(st + 1) * 128],
                    env["identf_sb"][0:64, 0:64]),
                    wait=([(ve, ddec_ve), (ve, env["tr_bank_ve"].get(("fr", 0), 0))]
                          if st == 0 else ()),
                    inc=((pe, 1) if st == 1 else None))
            rec.op("vector", lambda e, dst=dst: e.tensor_copy(dst[:], ps_fra[:, 0:128]),
                   wait=[(pe, pe.v)], inc=(ve, 1), flush=vf)
            env["tr_bank_ve"][("fr", 0)] = ve.v
        env[(tg, "dt_tr_pe")] = pe.v
        sptm_ve = ve.v
        rec.op("tensor", lambda e: e.transpose(
            ps_frb[0:1, 0:64], acumf[:, 255:256], env["identf_sb"][0:64, 0:64]),
            wait=[(ve, acumf_ve), (ve, env["tr_bank_ve"].get(("fr", 1), 0)),
                  (ve, env.get((tg, "erow_ve"), 0))],
            inc=(pe, 1))
        rec.op("vector", lambda e: e.tensor_copy(arow[:], ps_frb[0:1, 0:64]),
               wait=[(pe, pe.v)], inc=(ve, 1), flush=vf)
        env["tr_bank_ve"][("fr", 1)] = ve.v
        rec.op("scalar", lambda e: e.activation(erow[:], arow[:], AF.Exp),
               wait=[(ve, ve.v)], inc=(ac, 1))
        erow_ac = ac.v
        rec.op("tensor", lambda e: e.matmul(ps_c[:, 0:64], env["ones_sb"][:], erow[:],
                                            start=True, stop=True),
               wait=[(ac, erow_ac), (ve, env.get((tg, "y_evac_ve", 0), 0))],
           inc=(pe, 1))
        rec.op("vector", lambda e: e.tensor_copy(e_tile[:], ps_c[:, 0:64]),
               wait=[(pe, pe.v)], inc=(ve, 1), flush=vf)
        env[(tg, "erow_ve")] = ve.v
        etile_ve = ve.v
        rec.op("vector", lambda e: e.tensor_copy(acumhi[:], acumf[:]),
               wait=[(ve, acumf_ve)], flush=vf)
        rec.op("vector", lambda e: e.tensor_sub(acumlo[:].bitcast(F32)[0:64, 0:128]
               if False else acumlo[:], acumf[:], acumhi[:]), flush=vf, inc=(ve, 1))
        hilo_ve = ve.v
        aw_ = [(ve, hilo_ve)]
        if (tg, "acr_ld_last") in env:
            aw_.append(env[(tg, "acr_ld_last")])
        dma(rec, "sync", s_acum[0], acumhi[:], st_as, wait=aw_)
        sacum_tok = dma(rec, "sync", s_acum[1], acumlo[:], st_as)

        if upto < 4:
            continue
        for st in range(2):
            rec.op("vector", lambda e, st=st: e.tensor_tensor(
                xdt_tm[st][:].rearrange("p (h d) -> p h d", h=64),
                X_tm[st][:].rearrange("p (h d) -> p h d", h=64),
                sp_tm[:, st * 64:(st + 1) * 64].rearrange("p (h one) -> p h one", one=1)
                    .broadcast_to([128, 64, 64]),
                ALU.mult),
                wait=[(ve, sptm_ve), (pe, env.get((tg, "ydiag_pe"), 0)),
                      (ac, env.get((tg, "gate_ac"), 0))])
            rec.op("vector", lambda e, st=st: e.tensor_tensor(
                Xdec_tm[st][:].rearrange("p (h d) -> p h d", h=64),
                X_tm[st][:].rearrange("p (h d) -> p h d", h=64),
                ddec_tm[:, st * 64:(st + 1) * 64].rearrange("p (h one) -> p h one", one=1)
                    .broadcast_to([128, 64, 64]),
                ALU.mult), inc=(ve, 1))
        env[(tg, "xdt_ve")] = ve.v
        xdt_ve = ve.v

        if upto < 5:
            continue
        scw = [(ac, silu_ac), (po, env.get((tg, "scm_cons_po"), 0))]
        rec.op("tensor", lambda e: e.matmul(ps_a[:, 0:256], xsbc[:, 32, 0:128],
                                            xsbc[:, 33, :], start=True, stop=True),
               wait=scw, inc=(pe, 1))
        sc0_pe = pe.v
        rec.op("tensor", lambda e: e.matmul(ps_d[:, 0:256], xsbc[:, 32, 128:256],
                                            xsbc[:, 33, :], start=True, stop=True),
               inc=(pe, 1))
        sc1_pe = pe.v
        rec.op("vector", lambda e: e.tensor_mul(scoresm[0][:], ps_a[:, 0:256],
                                                trim_sb[:, 0:256]),
               wait=[(pe, sc0_pe)], inc=(ve, 1), flush=vf)
        rec.op("vector", lambda e: e.tensor_mul(scoresm[1][:], ps_d[:, 0:256],
                                                trim_sb[:, 256:512]),
               wait=[(pe, sc1_pe)], inc=(ve, 1), flush=vf)
        scores_ve = ve.v

        if upto < 6:
            continue
        rec.op("vector", lambda e: e.tensor_tensor(
            S[:].rearrange("p (h d) -> p h d", h=64),
            S[:].rearrange("p (h d) -> p h d", h=64),
            e_tile[:].rearrange("p (h one) -> p h one", one=1)
                .broadcast_to([128, 64, 64]),
            ALU.mult),
            wait=[(ve, etile_ve), (ve, env.get((tg, "sbf_ve"), 0))], flush=vf,
            inc=(ve, 1))
        smul_ve = ve.v
        st_banks = [ps_d, ps_b]
        for sl in range(8):
            q = sl % 2
            stw = [(ve, xdt_ve), (ve, btm_ve), (ve, scores_ve), (ve, acumtm_ve),
                   (ve, env.get((tg, "sadd_ve", q), 0))]
            rec.op("tensor", lambda e, sl=sl, q=q: e.matmul(
                st_banks[q][:], B_tm[:, 0:128], Xdec_tm[0][:, sl * 512:(sl + 1) * 512],
                start=True, stop=False), wait=stw)
            rec.op("tensor", lambda e, sl=sl, q=q: e.matmul(
                st_banks[q][:], B_tm[:, 128:256], Xdec_tm[1][:, sl * 512:(sl + 1) * 512],
                start=False, stop=True), inc=(pe, 1))
            st_pe = pe.v
            rec.op("vector", lambda e, sl=sl, q=q: e.tensor_add(
                S[:, sl * 512:(sl + 1) * 512], S[:, sl * 512:(sl + 1) * 512],
                st_banks[q][:]),
                wait=[(pe, st_pe), (ve, smul_ve)], inc=(ve, 1))
            env[(tg, "sadd_ve", q)] = ve.v
        supd_ve = ve.v

        gw_ = [(pe, env[(tg, "xtr_pe")])]
        if (tg, "ynT_so") in env:
            gw_.append(env[(tg, "ynT_so")])
        for q in range(4):
            gate_tok = dma(rec, "sync", gate_fm[:, q * 8:(q + 1) * 8, :],
                           s_gateT[q * 8:(q + 1) * 8, :, t0:t0 + 256].rearrange("c p t -> p c t"),
                           st_gt, wait=(gw_ if q == 0 else ()))
        for st in range(2):
            trv, trpe = emit_transpose_pack(
                rec, env,
                src_ap_fn=lambda j, st=st: gate_fm[:, j, st * 128:(st + 1) * 128],
                nblk=32,
                dst_ap_fn=lambda b, nb, st=st: gate_tm[st][:, b * 1024:b * 1024 + nb * 128]
                    .rearrange("p (j c) -> p j c", j=nb),
                bf=True, wait_first=[gate_tok, (ve, supd_ve)])
        gate_tr_ve = trv
        gate_tr_pe = trpe
        env[(tg, "xtr_pe")] = max(env.get((tg, "xtr_pe"), 0), gate_tr_pe)

        if upto < 7:
            continue
        y_bank = [ps_c, ps_a]
        for h in range(nheads):
            g8 = h // 8
            par = h % 2
            if h % 8 == 0:
                aw8 = [sacum_tok, (pe, env.get((tg, "t1_pe", g8 % 2), 0))]
                hi_tok = dma(rec, "sync", acrowh[g8 % 2][:],
                             s_acum[0].rearrange("h l -> (h l)").rearrange(
                                 "(one f) -> one f", one=1)[0:1, g8 * 2048:(g8 + 1) * 2048],
                             st_arh, wait=aw8)
                lo_tok = dma(rec, "sync", acrowl[g8 % 2][:],
                             s_acum[1].rearrange("h l -> (h l)").rearrange(
                                 "(one f) -> one f", one=1)[0:1, g8 * 2048:(g8 + 1) * 2048],
                             st_arl)
                env[(tg, "acr_ld", g8 % 2)] = (hi_tok, lo_tok)
                env[(tg, "acr_ld_last")] = lo_tok
            if h % 2 == 0:
                t1w = list(env[(tg, "acr_ld", g8 % 2)]) + [
                       (ac, env.get((tg, "ehb_ac", 1), 0)),
                       (ac, env.get((tg, "exn_ac", 1), 0))]
                if h == 0:
                    t1w.append((ve, acumtm_ve))
                    t1w.append((ve, env.get((tg, "sadd_ve", 1), 0)))
                rec.op("tensor", lambda e, h=h, g8=g8: e.matmul(
                    ps_b[:], ones_bf[:],
                    acrowh[g8 % 2][0:1, (h % 8) * 256:(h % 8) * 256 + 512],
                    start=True, stop=False),
                    wait=t1w)
                rec.op("tensor", lambda e, h=h, g8=g8: e.matmul(
                    ps_b[:], ones_bf[:],
                    acrowl[g8 % 2][0:1, (h % 8) * 256:(h % 8) * 256 + 512],
                    start=False, stop=True),
                    inc=(pe, 1))
                env[(tg, "t1_pe", g8 % 2)] = pe.v
                t1_pe = pe.v
            half = (h % 2) * 256
            for st in range(2):
                rec.op("scalar", lambda e, st=st, half=half, h=h, par=par: e.activation(
                    ehb[par][:, st * 256:(st + 1) * 256], ps_b[:, half:half + 256],
                    AF.Exp, bias=nacum_tm[:, st * 64 + h:st * 64 + h + 1]),
                    wait=([(pe, t1_pe), (ve, acumtm_ve),
                           (ve, env.get((tg, "mh_ve", par), 0)),
                           (po, env.get((tg, "mh_po", par), 0))] if st == 0 else ()),
                    inc=((ac, 1) if st == 1 else None))
            env[(tg, "ehb_ac", par)] = ac.v
            ehb_ac = ac.v
            rec.op("vector", lambda e, par=par: e.scalar_tensor_tensor(
                Mh[par][0][:], ehb[par][:, 0:256], 1.0, scoresm[0][:],
                ALU.min, ALU.mult),
                wait=[(ac, ehb_ac), (pe, env.get((tg, "ymm_pe", par), 0))],
                inc=(ve, 1))
            env[(tg, "mh_ve", par)] = ve.v
            rec.op("vector", lambda e, par=par: e.scalar_tensor_tensor(
                Mh[par][1][:], ehb[par][:, 256:512], 1.0, scoresm[1][:],
                ALU.min, ALU.mult),
                wait=[(ac, ehb_ac), (pe, env.get((tg, "ymm_pe", par), 0))],
                inc=(ve, 1))
            env[(tg, "mh_po", par)] = 0
            mh_po = 0
            env[(tg, "mh_ve", par)] = ve.v
            mh_ve = env[(tg, "mh_ve", par)]
            rec.op("scalar", lambda e, half=half, par=par: e.activation(
                exn[par][:], ps_b[:, half:half + 256], AF.Exp),
                wait=[(pe, t1_pe), (ve, env.get((tg, "chp_ve", par), 0))], inc=(ac, 1))
            env[(tg, "exn_ac", par)] = ac.v
            rec.op("vector", lambda e, par=par: e.tensor_mul(
                chp[par][:], xsbc[:, 33, :], exn[par][:]),
                wait=[(ac, ac.v), (pe, env.get((tg, "ymm_pe", par), 0))], inc=(ve, 1))
            env[(tg, "chp_ve", par)] = ve.v
            chp_ve = ve.v
            yb = y_bank[par]
            yw0 = [(po, mh_po), (ve, mh_ve), (ve, chp_ve),
                   (ve, env.get((tg, "y_evac_ve", par), 0))]
            if h == 0:
                yw0.append((ve, etile_ve))
            if h == 1:
                yw0.append((ve, scores_ve))
            for lt in range(2):
                rec.op("tensor", lambda e, lt=lt, h=h, yb=yb, par=par: e.matmul(
                    yb[:, lt * 64:(lt + 1) * 64], Mh[par][0][:, lt * 128:(lt + 1) * 128],
                    xdt_tm[0][:, h * 64:(h + 1) * 64], start=True, stop=False),
                    wait=(yw0 if lt == 0 else ()))
                rec.op("tensor", lambda e, lt=lt, h=h, yb=yb, par=par: e.matmul(
                    yb[:, lt * 64:(lt + 1) * 64], Mh[par][1][:, lt * 128:(lt + 1) * 128],
                    xdt_tm[1][:, h * 64:(h + 1) * 64], start=False, stop=False))
                rec.op("tensor", lambda e, lt=lt, h=h, yb=yb, par=par: e.matmul(
                    yb[:, lt * 64:(lt + 1) * 64], chp[par][:, lt * 128:(lt + 1) * 128],
                    S_bf[:, h * 64:(h + 1) * 64], start=False, stop=True),
                    inc=((pe, 1) if lt == 1 else None))
            env[(tg, "ymm_pe", par)] = pe.v
            env[(tg, "ydiag_pe")] = pe.v
            ymm_pe = pe.v
            for lt in range(2):
                rec.op("vector", lambda e, lt=lt, h=h, yb=yb: e.scalar_tensor_tensor(
                    y_tm[lt][:, h * 64:(h + 1) * 64],
                    X_tm[lt][:, h * 64:(h + 1) * 64],
                    D_tile[:, h:h + 1],
                    yb[:, lt * 64:(lt + 1) * 64],
                    ALU.mult, ALU.add),
                    wait=([(pe, ymm_pe), (ve, env.get((tg, "yn_ve"), 0))]
                          if lt == 0 else ()),
                    inc=((ve, 1) if lt == 1 else None))
            env[(tg, "y_evac_ve", par)] = ve.v
        heads_ve = ve.v
        heads_pe = pe.v
        heads_po = po.v
        env[(tg, "scm_cons_po")] = po.v
        if not do_post:
            continue

        rec.op("vector", lambda e: e.tensor_copy(S_bf[:], S[:]),
               wait=[(pe, heads_pe)], inc=(ve, 1))
        sbf_ve = ve.v
        env[(tg, "sbf_ve")] = sbf_ve

        for lt in range(2):
            if V == "l0":
                for sub in range(2):
                    rec.op("scalar", lambda e, lt=lt, sub=sub: e.activation(
                        sil_half[:], gate_tm[lt][:, sub * 2048:(sub + 1) * 2048],
                        AF.Silu),
                        wait=[(ve, gate_tr_ve), (ve, env.get((tg, "sil_cons_ve"), 0))],
                        inc=(ac, 1))
                    rec.op("vector", lambda e, lt=lt, sub=sub: e.tensor_mul(
                        y_tm[lt][:, sub * 2048:(sub + 1) * 2048],
                        y_tm[lt][:, sub * 2048:(sub + 1) * 2048],
                        sil_half[:]),
                        wait=[(ac, ac.v), (ve, heads_ve)], flush=vf, inc=(ve, 1))
                    env[(tg, "sil_cons_ve")] = ve.v
                yg_ve = ve.v
                rec.op("scalar", lambda e, lt=lt: e.activation(
                    env["trash"][:, 0:2048], y_tm[lt][:, 0:2048], AF.Square,
                    accum_out=vc0[:]),
                    wait=[(ve, yg_ve), (ve, env.get((tg, "vc_cons_ve"), 0))], flush=af)
                rec.op("scalar", lambda e, lt=lt: e.activation(
                    env["trash"][:, 0:2048], y_tm[lt][:, 2048:4096], AF.Square,
                    accum_out=vc1[:]), flush=af, inc=(ac, 1))
                sq_ac = ac.v
                rec.op("vector", lambda e: e.tensor_add(vc0[:], vc0[:], vc1[:]),
                       wait=[(ac, sq_ac)], flush=vf)
                rec.op("vector", lambda e: e.tensor_scalar(vc0[:], vc0[:], 1.0 / INTER,
                                                           None, ALU.mult),
                       flush=vf, inc=(ve, 1))
                rec.op("scalar", lambda e: e.activation(vc1[:], vc0[:], AF.Ln,
                                                        bias=env["eps_col"][:]),
                       wait=[(ve, ve.v)], flush=af)
                rec.op("scalar", lambda e: e.activation(vc1[:], vc1[:], AF.Exp,
                                                        scale=-0.5),
                       flush=af, inc=(ac, 1))
                rstd_ac = ac.v
                env[(tg, "rms_ac")] = rstd_ac
                env[(tg, "gate_ac")] = max(env.get((tg, "gate_ac"), 0), ac.v)
                rec.op("vector", lambda e, lt=lt: e.tensor_scalar(
                    X_tm[lt][:], y_tm[lt][:], vc1[:], None, ALU.mult),
                    wait=[(ac, rstd_ac)], inc=(ve, 1))
                env[(tg, "vc_cons_ve")] = ve.v
            else:
                rec.op("scalar", lambda e, lt=lt: e.activation(
                    env["trash"][:, 0:2048], y_tm[lt][:, 0:2048], AF.Square,
                    accum_out=vc0[:]),
                    wait=[(ve, heads_ve), (ve, env.get((tg, "vc_cons_ve"), 0))], flush=af)
                rec.op("scalar", lambda e, lt=lt: e.activation(
                    env["trash"][:, 0:2048], y_tm[lt][:, 2048:4096], AF.Square,
                    accum_out=vc1[:]), flush=af, inc=(ac, 1))
                sq_ac = ac.v
                rec.op("vector", lambda e: e.tensor_add(vc0[:], vc0[:], vc1[:]),
                       wait=[(ac, sq_ac)], flush=vf)
                rec.op("vector", lambda e: e.tensor_scalar(vc0[:], vc0[:], 1.0 / INTER,
                                                           None, ALU.mult),
                       flush=vf, inc=(ve, 1))
                rec.op("scalar", lambda e: e.activation(vc1[:], vc0[:], AF.Ln,
                                                        bias=env["eps_col"][:]),
                       wait=[(ve, ve.v)], flush=af)
                rec.op("scalar", lambda e: e.activation(vc1[:], vc1[:], AF.Exp,
                                                        scale=-0.5),
                       flush=af, inc=(ac, 1))
                rstd_ac = ac.v
                for sub in range(2):
                    rec.op("scalar", lambda e, lt=lt, sub=sub: e.activation(
                        sil_half[:], gate_tm[lt][:, sub * 2048:(sub + 1) * 2048],
                        AF.Silu),
                        wait=[(ve, gate_tr_ve), (ve, env.get((tg, "sil_cons_ve"), 0)),
                              (ac, rstd_ac)],
                        inc=(ac, 1))
                    rec.op("vector", lambda e, lt=lt, sub=sub: e.scalar_tensor_tensor(
                        X_tm[lt][:, sub * 2048:(sub + 1) * 2048],
                        sil_half[:], vc1[:],
                        y_tm[lt][:, sub * 2048:(sub + 1) * 2048],
                        ALU.mult, ALU.mult),
                        wait=[(ac, ac.v)], inc=(ve, 1))
                    env[(tg, "sil_cons_ve")] = ve.v
                    env[(tg, "vc_cons_ve")] = ve.v
                env[(tg, "rms_ac")] = ac.v
                env[(tg, "gate_ac")] = max(env.get((tg, "gate_ac"), 0), ac.v)
        yn_ve = ve.v
        env[(tg, "yn_ve")] = yn_ve
        ytw = [(ve, yn_ve)]
        if (tg, "ynT_so") in env:
            ytw.append(env[(tg, "ynT_so")])
        for lt in range(2):
            trv, trpe = emit_transpose_pack(
                rec, env,
                src_ap_fn=lambda j, lt=lt: X_tm[lt][:, j * 128:(j + 1) * 128],
                nblk=32,
                dst_ap_fn=lambda b, nb, lt=lt: ynT[:, b * 8:b * 8 + nb,
                                                   lt * 128:(lt + 1) * 128],
                bf=True, wait_first=ytw)
        for q in range(4):
            yn_tok = dma(rec, "sync",
                         s_ynT[q * 8:(q + 1) * 8, :, t0:t0 + 256].rearrange("c p t -> p c t"),
                         ynT[:, q * 8:(q + 1) * 8, :], st_ys,
                         wait=([(ve, trv)] if q == 0 else ()))
        env[(tg, "ynT_so")] = yn_tok
    scan_tok = env.get((tg, "ynT_so"))
    scan_pe = pe.v
    scan_ve = ve.v
    if phases == "ab":
        if scan_tok:
            rec.wait("sync", scan_tok[0], scan_tok[1])
        for c_, v_ in phA_tok:
            rec.wait("sync", c_, v_)
        rec.wait("vector", ve, ve.v)
        return

    for ns in range(4):
        for kb in range(32):
            wlw = [(ve, env.get((tg, "wkf_ve", kb % 2), 0))]
            if ns == 0 and kb < 2:
                wlw.append((ve, scan_ve))
            wk_tok = dma(rec, "sync", wkf[kb % 2][:],
                         w_out[kb * 128:(kb + 1) * 128, ns * 512:(ns + 1) * 512],
                         st_wk, wait=wlw)
            wtw = [wk_tok, (pe, env.get((tg, "wns_pe"), 0))]
            if ns == 0 and kb == 0:
                wtw += ([scan_tok] if scan_tok else []) + \
                       [(ve, scan_ve), (ac, env.get((tg, "rms_ac"), 0))]
            rec.op("vector", lambda e, kb=kb, ns=ns: e.tensor_scalar(
                wns[:, kb, :], wkf[kb % 2][:], nw_sb[:, kb:kb + 1], None, ALU.mult),
                wait=wtw, inc=(ve, 1))
            env[(tg, "wkf_ve", kb % 2)] = ve.v
        wns_ve = ve.v
        for tt in range(nt):
            gidx = ns * nt + tt
            t0 = tt * 128
            ytw = ([scan_tok] if scan_tok else []) if gidx < 2 else \
                  [(pe, env.get((tg, "ynTt_pe", gidx % 2), 0))]
            for q in range(4):
                yt_tok = dma(rec, "sync", ynTt[gidx % 2][:, q * 8:(q + 1) * 8, :],
                             s_ynT[q * 8:(q + 1) * 8, :, t0:t0 + 128].rearrange("c p t -> p c t"),
                             st_yt, wait=(ytw if q == 0 else ()))
            xw_ = [(ve, env.get((tg, "xcs_ve", gidx % 2), 0))]
            if gidx < 2:
                xw_.append((ve, scan_ve))
            xc_tok = dma(rec, "sync", xcs[gidx % 2][:],
                         x[t0:t0 + 128, ns * 512:(ns + 1) * 512], st_xc, wait=xw_)
            psm = [ps_a, ps_b][gidx % 2]
            mw = [yt_tok, xc_tok, (ve, wns_ve), (ve, env.get((tg, "out_ve", gidx % 2), 0))]
            for kb in range(32):
                rec.op("tensor", lambda e, kb=kb, ns=ns, gidx=gidx, psm=psm: e.matmul(
                    psm[:], ynTt[gidx % 2][:, kb, :], wns[:, kb, :],
                    start=(kb == 0), stop=(kb == 31)),
                    wait=(mw if kb == 0 else ()),
                    inc=((pe, 1) if kb == 31 else None))
            env[(tg, "ynTt_pe", gidx % 2)] = pe.v
            env[(tg, "wns_pe")] = pe.v
            omm_pe = pe.v
            ow_ = [(pe, omm_pe)]
            if (tg, "out_so", gidx % 2) in env:
                ow_.append(env[(tg, "out_so", gidx % 2)])
            rec.op("vector", lambda e, gidx=gidx, psm=psm: e.tensor_add(
                outs[gidx % 2][:], psm[:], xcs[gidx % 2][:]),
                wait=ow_, inc=(ve, 1))
            env[(tg, "xcs_ve", gidx % 2)] = ve.v
            env[(tg, "out_ve", gidx % 2)] = ve.v
            env[(tg, "out_so", gidx % 2)] = dma(
                rec, "sync", xo[t0:t0 + 128, ns * 512:(ns + 1) * 512],
                outs[gidx % 2][:], st_os, wait=[(ve, ve.v)])

    for cnt in st_os.cnts:
        rec.wait("sync", cnt, cnt.v)


def build_mamba(variant, nt=NT, dbg=False, phases="abc", scan_chunks=None, nheads=NH, do_post=True, upto=99):
    nc = bass.Bass("TRN2", target_bir_lowering=False, debug=False)
    kind_s = "ExternalOutput" if dbg else "Internal"
    aps = {
        "x": nc.dram_tensor("x", [SEQ, D], F32, kind="ExternalInput").ap(),
        "w_in": nc.dram_tensor("w_in", [D, 8512], F32, kind="ExternalInput").ap(),
        "conv_w": nc.dram_tensor("conv_w", [CDIM, 4], F32, kind="ExternalInput").ap(),
        "conv_b": nc.dram_tensor("conv_b", [CDIM], F32, kind="ExternalInput").ap(),
        "dt_bias": nc.dram_tensor("dt_bias", [64], F32, kind="ExternalInput").ap(),
        "a_log": nc.dram_tensor("a_log", [64], F32, kind="ExternalInput").ap(),
        "dvec": nc.dram_tensor("dvec", [64], F32, kind="ExternalInput").ap(),
        "norm_w": nc.dram_tensor("norm_w", [INTER], F32, kind="ExternalInput").ap(),
        "w_out": nc.dram_tensor("w_out", [INTER, D], F32, kind="ExternalInput").ap(),
        "xo": nc.dram_tensor("xo", [SEQ, D], F32, kind="ExternalOutput").ap(),
        "s_hsT": nc.dram_tensor("s_hsT", [HS_T, 128, SEQ], BF16, kind=kind_s).ap(),
        "s_gateT": nc.dram_tensor("s_gateT", [GATE_T, 128, SEQ], BF16, kind=kind_s).ap(),
        "s_dt": nc.dram_tensor("s_dt", [64, SEQ], F32, kind=kind_s).ap(),
        "s_ynT": nc.dram_tensor("s_ynT", [GATE_T, 128, SEQ], BF16, kind=kind_s).ap(),
        "s_acum": nc.dram_tensor("s_acum", [2, 64, 256], BF16, kind=kind_s).ap(),
    }
    ident_bf, identf, ones_f = mk_consts(nc)
    cst = mamba_consts(nc, "mb")
    ctx = ExitStack()
    with ctx:
        env, sems, sb, psa = common_env(nc, ctx)
        rec = Rec()
        emit_load_consts(rec, env, ident_bf, identf, ones_f)
        emit_mamba(rec, env, aps, sb, psa, variant, cst, nt=nt, phases=phases, scan_chunks=scan_chunks, nheads=nheads, do_post=do_post, upto=upto)
        rec.check()
        block = ctx.enter_context(nc.Block())
        rec.replay(block)
    return nc


ARENA_BYTES = 206336


def build_fused(nt=NT):
    nc = bass.Bass("TRN2", target_bir_lowering=False, debug=False)
    dt_ = lambda n, s, k="ExternalInput", d=F32: nc.dram_tensor(n, s, d, kind=k).ap()
    xo = dt_("xo", [SEQ, D], "ExternalOutput")
    x_in = dt_("x", [SEQ, D])
    x1 = dt_("x1", [SEQ, D], "Internal")
    x2 = dt_("x2", [SEQ, D], "Internal")
    x3 = dt_("x3", [SEQ, D], "Internal")

    def mamba_aps(pref, xi, xo_):
        return {
            "x": xi, "xo": xo_,
            "w_in": dt_(pref + "w_in", [D, 8512]),
            "conv_w": dt_(pref + "conv_w", [CDIM, 4]),
            "conv_b": dt_(pref + "conv_b", [CDIM]),
            "dt_bias": dt_(pref + "dt_bias", [64]),
            "a_log": dt_(pref + "a_log", [64]),
            "dvec": dt_(pref + "dvec", [64]),
            "norm_w": dt_(pref + "norm_w", [INTER]),
            "w_out": dt_(pref + "w_out", [INTER, D]),
            "s_hsT": dt_(pref + "s_hsT", [HS_T, 128, SEQ], "Internal", BF16),
            "s_gateT": dt_(pref + "s_gateT", [GATE_T, 128, SEQ], "Internal", BF16),
            "s_dt": dt_(pref + "s_dt", [64, SEQ], "Internal"),
            "s_ynT": dt_(pref + "s_ynT", [GATE_T, 128, SEQ], "Internal", BF16),
            "s_acum": dt_(pref + "s_acum", [2, 64, 256], "Internal", BF16),
        }

    aps_l0 = mamba_aps("l0_", x_in, x1)
    aps_f0 = {"x": x1, "xo": x2, "w": dt_("f0_w", [D, D]), "bias": dt_("f0_bias", [D])}
    aps_l1 = mamba_aps("l1_", x2, x3)
    aps_m = {"x": x3, "xo": xo,
             "gwt": dt_("m_gwt", [D, 8]),
             "w1t": dt_("m_w1t", [NE, D, MI]), "w3t": dt_("m_w3t", [NE, D, MI]),
             "w2t": dt_("m_w2t", [NE, MI, D]),
             "sw1t": dt_("m_sw1t", [D, 2 * MI]), "sw3t": dt_("m_sw3t", [D, 2 * MI]),
             "sw2t": dt_("m_sw2t", [2 * MI, D]),
             "s_xlnT": dt_("m_s_xlnT", [NT, 128, D], "Internal", BF16),
             "which": None}

    ident_bf, identf, ones_f = mk_consts(nc)
    cst = mamba_consts(nc, "mb")
    ctx = ExitStack()
    with ctx:
        env, sems, sb, psa = common_env(nc, ctx)
        arena_h = ctx.enter_context(nc.sbuf_tensor("phase_arena", [128, ARENA_BYTES],
                                                    mybir.dt.uint8))
        pa = PhaseAlloc(arena_h.ap(), ARENA_BYTES)
        rec = Rec()
        emit_load_consts(rec, env, ident_bf, identf, ones_f)
        emit_mamba(rec, env, aps_l0, pa.sb, psa, "l0", cst, nt=nt)
        emit_barrier(rec, env)
        pa.reset()
        emit_ffn0(rec, env, aps_f0, pa.sb, psa)
        emit_barrier(rec, env)
        pa.reset()
        emit_mamba(rec, env, aps_l1, pa.sb, psa, "l1", cst, nt=nt)
        emit_barrier(rec, env)
        pa.reset()
        emit_moe(rec, env, aps_m, pa.sb, psa, nt=nt)
        emit_barrier(rec, env)
        rec.check()
        block = ctx.enter_context(nc.Block())
        rec.replay(block)
    return nc



_BUILD_CACHE = {}


def _get(name, fn):
    if name not in _BUILD_CACHE:
        _BUILD_CACHE[name] = fn()
    return _BUILD_CACHE[name]


def kernel(**inputs):
    z = inputs
    ncore = 8
    c = np.ascontiguousarray
    x = c(z["L0_attn_x"]).astype(np.float32, copy=False)
    nc = _get("fused", build_fused)
    shared = {
        "l0_w_in": z["L0_attn_in_proj_w"],
        "l0_conv_w": z["L0_attn_conv_w"],
        "l0_conv_b": z["L0_attn_conv_b"],
        "l0_dt_bias": z["L0_attn_dt_bias"],
        "l0_a_log": z["L0_attn_A_log"],
        "l0_dvec": z["L0_attn_D"],
        "l0_norm_w": z["L0_attn_norm_w"],
        "l0_w_out": z["L0_attn_out_proj_w"],
        "f0_w": z["L0_ffn_weight"],
        "f0_bias": z["L0_ffn_bias"],
        "l1_w_in": c(z["L1_attn_in_proj_w"].T),
        "l1_conv_w": c(z["L1_attn_conv_w"].reshape(CDIM, 4)),
        "l1_conv_b": z["L1_attn_conv_b"],
        "l1_dt_bias": z["L1_attn_dt_bias"],
        "l1_a_log": z["L1_attn_A_log"],
        "l1_dvec": z["L1_attn_D"],
        "l1_norm_w": z["L1_attn_norm_w"],
        "l1_w_out": c(z["L1_attn_out_proj_w"].T),
        "m_gwt": c(z["L1_ffn_gate_w"].T),
        "m_w1t": c(np.transpose(z["L1_ffn_w1"], (0, 2, 1))),
        "m_w3t": c(np.transpose(z["L1_ffn_w3"], (0, 2, 1))),
        "m_w2t": c(np.transpose(z["L1_ffn_w2"], (0, 2, 1))),
        "m_sw1t": c(z["L1_ffn_sw1"].T),
        "m_sw3t": c(z["L1_ffn_sw3"].T),
        "m_sw2t": c(z["L1_ffn_sw2"].T),
    }
    in_maps = [dict(shared, x=x[b]) for b in range(ncore)]
    res = run_bass_kernel_spmd(nc, in_maps, core_ids=list(range(ncore)))
    return np.stack([r["xo"] for r in res.results], axis=0)
